# version 14
# baseline (speedup 1.0000x reference)
import math
import numpy as np
import ml_dtypes
import concourse.bass as bass
import concourse.mybir as mybir
from concourse.bass_utils import run_bass_kernel_spmd

F32 = mybir.dt.float32
BF16 = mybir.dt.bfloat16
AF = mybir.ActivationFunctionType
ALU = mybir.AluOpType
AX = mybir.AxisListType

B, T, D = 4, 4096, 2048
H, HD = 16, 128
NE, MD = 16, 1408
MT = MD // 128
TOPK = 6
ATTN_MULT = 1.0 / math.sqrt(HD)
NEG_BIG = -1e30 / ATTN_MULT
EPS = 1e-5
OWN = 2048
NTT = OWN // 128
NDT = D // 128

BF = ml_dtypes.bfloat16


class Emitter:
    ENGINES = ("t", "v", "s", "q", "g")

    def __init__(self, nc, sem_names):
        self.nc = nc
        self.sems = {n: nc.alloc_semaphore(n) for n in sem_names}
        self.cnt = {n: 0 for n in sem_names}
        self.ops = {e: [] for e in self.ENGINES}
        self.free = {}

    def w(self, e, sem, val=None):
        v = self.cnt[sem] if val is None else val
        if v is None or v <= 0:
            return
        h = self.sems[sem]
        self.ops[e].append(lambda eng, h=h, v=v: eng.wait_ge(h, v))

    def do(self, e, fn, inc=None, n=1):
        if inc is None:
            self.ops[e].append(fn)
            return None
        self.cnt[inc] += n
        h = self.sems[inc]
        v = self.cnt[inc]
        self.ops[e].append(lambda eng, fn=fn, h=h, n=n: fn(eng).then_inc(h, n))
        return v

    def mark(self, res, sem, val):
        self.free[res] = (sem, val)

    def gate(self, e, res):
        if res in self.free:
            sem, val = self.free[res]
            self.w(e, sem, val)

    def begin_defer(self, e):
        self._saved = self.ops[e]
        self.ops[e] = []

    def end_defer(self, e):
        cap = self.ops[e]
        self.ops[e] = self._saved
        return cap

    def barrier(self):
        snapshot = [(n, self.cnt[n]) for n in self.sems if self.cnt[n] > 0]
        for e in self.ENGINES:
            for n, v in snapshot:
                self.w(e, n, v)


def build_program():
    nc = bass.Bass("TRN2", target_bir_lowering=False)

    x_in = nc.dram_tensor("x_b", [T, D], F32, kind="ExternalInput").ap()
    x_own = nc.dram_tensor("x_own", [OWN, D], F32, kind="ExternalInput").ap()
    wqkv = nc.dram_tensor("wqkv", [128, 24 * NDT * 256], F32, kind="ExternalInput").ap()
    bqkv = nc.dram_tensor("bqkv", [128, 48], F32, kind="ExternalInput").ap()
    wproj = nc.dram_tensor("wproj", [NDT, 128, D], F32, kind="ExternalInput").ap()
    bproj = nc.dram_tensor("bproj", [1, D], F32, kind="ExternalInput").ap()
    gwt = nc.dram_tensor("gwt", [128, NDT * NE], F32, kind="ExternalInput").ap()
    wg_t = nc.dram_tensor("wg_t", [NE, 128, NDT * MD], F32, kind="ExternalInput").ap()
    wu_t = nc.dram_tensor("wu_t", [NE, 128, NDT * MD], F32, kind="ExternalInput").ap()
    wd_t = nc.dram_tensor("wd_t", [NE, 128, 2 * MT * 1024], F32, kind="ExternalInput").ap()
    masks = nc.dram_tensor("masks", [128, NTT * 1024], BF16, kind="ExternalInput").ap()
    ident_in = nc.dram_tensor("ident", [128, 128], BF16, kind="ExternalInput").ap()
    out = nc.dram_tensor("out", [OWN, D], F32, kind="ExternalOutput").ap()

    qt_s = nc.dram_tensor("qt_s", [H, 128, OWN], BF16, kind="Internal").ap()
    kt_s = nc.dram_tensor("kt_s", [H, 128, T], BF16, kind="Internal").ap()
    vt_s = nc.dram_tensor("vt_s", [H, 128, T], BF16, kind="Internal").ap()
    x1_s = nc.dram_tensor("x1_s", [NTT, 128, D], F32, kind="Internal").ap()
    ht_s = nc.dram_tensor("ht_s", [NDT, 128, OWN], BF16, kind="Internal").ap()
    yt_s = nc.dram_tensor("yt_s", [H, NTT, 128, 128], BF16, kind="Internal").ap()

    SBASE = 16512
    def sbt(name, shape, dtype, off):
        return nc.alloc_sbuf_tensor_at(name, shape, dtype, offset=SBASE + off).ap()

    ident = sbt("identb", [128, 128], BF16, 0)
    bq_sb = sbt("bq_sb", [128, 48], F32, 256)
    gwt_sb = sbt("gwt_sb", [128, NDT * NE], BF16, 448)
    cw_sb = sbt("cw_sb", [128, NTT * NE], F32, 960)
    recip_sb = sbt("recip_sb", [128, NTT], F32, 1984)
    st_sums = sbt("st_sums", [128, 1], F32, 2048)
    st_ssq = sbt("st_ssq", [128, 1], F32, 2080)
    st_nm = sbt("st_nm", [128, 1], F32, 2112)
    st_var = sbt("st_var", [128, 1], F32, 2144)
    st_msq = sbt("st_msq", [128, 1], F32, 2176)
    st_lnv = sbt("st_lnv", [128, 1], F32, 2208)
    st_rstd = sbt("st_rstd", [128, 1], F32, 2240)
    st_stot = sbt("st_stot", [128, 1], F32, 2272)
    mtmp = sbt("mtmp", [128, 8], F32, 2304)
    rall = sbt("rall", [128, 8], F32, 2336)
    negr = sbt("negr", [128, 8], F32, 2368)
    chsum = sbt("chsum", [128, 8], F32, 2400)
    fixall = sbt("fixall", [128, 8], F32, 2432)
    fsum = sbt("fsum", [128, 8], F32, 2464)
    rt_sm = sbt("rt_sm", [128, NE], F32, 2496)
    rt_wk = sbt("rt_wk", [128, NE], F32, 2560)
    rt_ge = sbt("rt_ge", [128, NE], F32, 2624)
    rt_m = sbt("rt_m", [128, 1], F32, 2688)
    rt_s = sbt("rt_s", [128, 1], F32, 2720)
    A0 = 4096

    ln1T = sbt("ln1T", [128, NDT * T], BF16, A0)
    p1_x = [sbt(f"p1x{i}", [128, D], F32, A0 + 131072 + i * 8192) for i in range(2)]
    p1_ln = [sbt(f"p1ln{i}", [128, D], BF16, A0 + 147456 + i * 4096) for i in range(2)]
    p1_junk = sbt("p1junk", [128, D], BF16, A0 + 155648)
    p1_w = [sbt(f"p1w{i}", [128, NDT * 256], BF16, A0 + 159744 + i * 8192) for i in range(2)]
    p1_st = [sbt(f"p1st{i}", [128, 512], BF16, A0 + 176128 + i * 1024) for i in range(4)]

    mask_sb = sbt("mask_sb", [128, NTT * 1024], BF16, A0)
    p2_ss = [sbt(f"p2ss{i}", [128, 1024], F32, A0 + 32768 + i * 4096) for i in range(4)]
    p2_kt = [sbt(f"p2kt{i}", [128, T], BF16, A0 + 65536 + i * 8192) for i in range(2)]
    p2_vt = [sbt(f"p2vt{i}", [128, T], BF16, A0 + 81920 + i * 8192) for i in range(2)]
    _p2qt = sbt("p2qt0", [128, OWN], BF16, A0 + 98304)
    p2_qt = [_p2qt, _p2qt]
    _p2v = sbt("p2v0", [128, T], BF16, A0 + 106496)
    p2_v = [_p2v, _p2v]
    p2_att = sbt("p2att", [128, 28672], BF16, A0 + 114688)
    p2_attT = sbt("p2attT", [128, 32 * 512], BF16, A0 + 172032)
    p2_yst = [sbt(f"p2yst{i}", [128, 512], BF16, A0 + 204800 + i * 1024) for i in range(2)]

    p3_wp = sbt("p3wp", [128, NDT * D], BF16, A0)
    p3_bp = sbt("p3bp", [128, D], F32, A0 + 65536)
    p3_yt = [sbt(f"p3yt{i}", [128, H * 128], BF16, A0 + 73728 + i * 4096) for i in range(2)]
    p3_x = [sbt(f"p3x{i}", [128, D], F32, A0 + 81920 + i * 8192) for i in range(2)]
    p3_x1 = [sbt(f"p3x1{i}", [128, D], F32, A0 + 98304 + i * 8192) for i in range(2)]

    p4_x = [sbt(f"p4x{i}", [128, D], F32, A0 + i * 8192) for i in range(2)]
    p4_h = [sbt(f"p4h{i}", [128, D], BF16, A0 + 16384 + i * 4096) for i in range(2)]
    p4_ht = [sbt(f"p4ht{i}", [128, NDT * 128], BF16, A0 + 24576 + i * 4096) for i in range(2)]
    p4_junk = sbt("p4junk", [128, D], BF16, A0 + 32768)

    p5_ht = sbt("p5ht", [128, NDT * 1024], BF16, A0)
    p5_z = sbt("p5z", [128, MT * 1024], BF16, A0 + 32768)
    p5_wg = [sbt(f"p5wg{i}", [128, NDT * 256], BF16, A0 + 55296 + i * 8192) for i in range(2)]
    p5_wu = [sbt(f"p5wu{i}", [128, NDT * 256], BF16, A0 + 71680 + i * 8192) for i in range(2)]
    p5_wd = [sbt(f"p5wd{i}", [128, MT * 1024], BF16, A0 + 88064 + i * 22528) for i in range(2)]
    p5_sg = [sbt(f"p5sg{i}", [128, 1024], BF16, A0 + 133120 + i * 2048) for i in range(2)]
    p5_ev = [sbt(f"p5ev{i}", [128, 1024], F32, A0 + 137216 + i * 4096) for i in range(3)]

    PA = nc.alloc_psum_tensor("PA", [128, 1024], F32).ap()
    PB = nc.alloc_psum_tensor("PB", [128, 1024], F32).ap()
    PS3 = nc.alloc_psum_tensor("PS3", [128, 1024], F32).ap()
    PY = nc.alloc_psum_tensor("PY", [128, 512], F32).ap()
    PTA = nc.alloc_psum_tensor("PTA", [128, 1024], BF16).ap()
    fbanks = [PA[:, 0:512], PA[:, 512:1024], PB[:, 0:512], PB[:, 512:1024]]
    tbanks = [PTA, PTA]
    sunits = [PA, PB, PS3]

    E = Emitter(nc, ["gld", "qld", "sst", "gst", "pe", "dv", "ac", "ast"])

    E.do("q", lambda q: q.dma_start(out=ident[:], in_=ident_in[:]), "qld", 16)
    E.do("q", lambda q: q.dma_start(out=bq_sb[:], in_=bqkv[:]), "qld", 16)
    E.do("g", lambda g: g.dma_start(out=gwt_sb[:], in_=gwt[:]), "gld", 16)
    const_gld = E.cnt["gld"]

    def emit_ln_tile(xap, xload_sem, xload_val, lnout, junk):
        E.w("v", xload_sem, xload_val)
        E.do("v", lambda v, xap=xap: v.reduce_sum(st_sums[:], xap, axis=AX.X), "dv")
        v1 = E.do("v", lambda v, xap=xap, junk=junk: v.scalar_tensor_tensor(
            junk, xap, 1.0, xap, ALU.mult, ALU.mult, accum_out=st_ssq[:]), "dv")
        E.w("v", "dv", v1)
        E.do("v", lambda v: v.tensor_scalar(st_nm[:], st_sums[:], -1.0 / D, None, ALU.mult), "dv")
        v2 = E.do("v", lambda v: v.tensor_scalar(
            st_var[:], st_ssq[:], 1.0 / D, EPS, ALU.mult, ALU.add), "dv")
        E.w("v", "dv", v2)
        v3 = E.do("v", lambda v: v.scalar_tensor_tensor(
            st_msq[:], st_nm[:], 1.0, st_nm[:], ALU.mult, ALU.mult), "dv")
        E.w("v", "dv", v3)
        v4 = E.do("v", lambda v: v.tensor_sub(st_var[:], st_var[:], st_msq[:]), "dv")
        E.w("s", "dv", v4)
        s1 = E.do("s", lambda s: s.activation(st_lnv[:], st_var[:], AF.Ln), "ac")
        E.w("s", "ac", s1)
        s2 = E.do("s", lambda s: s.activation(st_rstd[:], st_lnv[:], AF.Exp, scale=-0.5), "ac")
        E.w("v", "ac", s2)
        vn = E.do("v", lambda v, xap=xap, lnout=lnout: v.tensor_scalar(
            lnout, xap, st_nm[:], st_rstd[:], ALU.add, ALU.mult), "dv")
        return vn

    lnv = {}
    for tt in range(T // 128):
        buf = tt % 2
        E.gate("q", ("p1x", buf))
        xv = E.do("q", lambda q, tt=tt, buf=buf: q.dma_start(
            out=p1_x[buf][:], in_=x_in[tt * 128:(tt + 1) * 128, :]), "qld", 16)
        E.gate("v", ("p1ln", buf))
        vn = emit_ln_tile(p1_x[buf][:], "qld", xv, p1_ln[buf][:], p1_junk[:])
        E.mark(("p1x", buf), "dv", vn)
        E.w("t", "dv", vn)
        E.w("t", "qld", 16)
        for half in range(2):
            bank = tbanks[half]
            E.gate("t", ("ptb",))
            last = None
            for k in range(8):
                dt = half * 8 + k
                last = E.do("t", lambda t, buf=buf, dt=dt, bank=bank, k=k: t.transpose(
                    bank[:, k * 128:(k + 1) * 128],
                    p1_ln[buf][:, dt * 128:(dt + 1) * 128], ident[:]),
                    "pe" if k == 7 else None)
            E.w("v", "pe", last)
            ev = E.do("v", lambda v, half=half, bank=bank, tt=tt: v.tensor_copy(
                ln1T.rearrange("p (dt c) -> p dt c", c=T)[:, half * 8:(half + 1) * 8,
                                                         tt * 128:(tt + 1) * 128],
                bank.rearrange("p (dt c) -> p dt c", c=128)), "dv")
            E.mark(("ptb",), "dv", ev)
            if half == 1:
                E.mark(("p1ln", buf), "dv", ev)
                lnv[tt] = ev

    E.w("v", "qld", 32)
    E.w("s", "qld", 32)
    ln3 = ln1T.rearrange("p (dt i c) -> p dt i c", i=T // 128, c=128)
    evac_eng = 0
    for grp in range(24):
        wbuf = grp % 2
        E.gate("g", ("p1w", wbuf))
        wv = E.do("g", lambda g, grp=grp, wbuf=wbuf: g.dma_start(
            out=p1_w[wbuf][:],
            in_=wqkv[:, grp * 4096:(grp + 1) * 4096]), "gld", 16)
        last_mm_of_group = None
        for jj in range(2):
            jt = grp * 2 + jj
            kind = jt // 16
            ntc = 4 if kind == 0 else 8
            for tc in range(ntc):
                bank_i = (jt * 8 + tc) % 4
                bank = fbanks[bank_i]
                E.gate("t", ("fb", bank_i))
                E.w("t", "gld", wv)
                need_tt = (8 * tc + 7) if kind == 0 else (4 * tc + 3)
                E.w("t", "dv", lnv[need_tt])
                last = None
                for dt in range(NDT):
                    if kind == 0:
                        rhs = ln3[:, dt, 8 * tc: 8 * tc + 8: 2, :]
                    else:
                        rhs = ln1T[:, dt * T + tc * 512: dt * T + (tc + 1) * 512]
                    last = E.do("t", lambda t, bank=bank, wbuf=wbuf, dt=dt, jj=jj,
                                rhs=rhs, first=(dt == 0), fin=(dt == NDT - 1):
                                t.matmul(bank, p1_w[wbuf][:, dt * 256 + jj * 128:
                                                          dt * 256 + jj * 128 + 128],
                                         rhs, start=first, stop=fin),
                                "pe" if dt == NDT - 1 else None)
                last_mm_of_group = last
                st_i = (jt * 8 + tc) % 4
                if kind == 0:
                    eng = "v"
                else:
                    eng = "s" if evac_eng % 2 == 0 else "v"
                    evac_eng += 1
                E.gate(eng, ("p1st", st_i))
                E.w(eng, "pe", last)
                if kind == 0:
                    evv = E.do("v", lambda v, st_i=st_i, bank=bank, jt=jt: v.tensor_scalar(
                        p1_st[st_i][:], bank, bq_sb[:, jt:jt + 1], ATTN_MULT,
                        ALU.add, ALU.mult), "dv")
                    E.mark(("fb", bank_i), "dv", evv)
                    E.w("q", "dv", evv)
                elif eng == "s":
                    evv = E.do("s", lambda s, st_i=st_i, bank=bank, jt=jt: s.add(
                        p1_st[st_i][:], bank, bq_sb[:, jt:jt + 1]), "ac")
                    E.mark(("fb", bank_i), "ac", evv)
                    E.w("q", "ac", evv)
                else:
                    evv = E.do("v", lambda v, st_i=st_i, bank=bank, jt=jt: v.tensor_scalar(
                        p1_st[st_i][:], bank, bq_sb[:, jt:jt + 1], None, ALU.add), "dv")
                    E.mark(("fb", bank_i), "dv", evv)
                    E.w("q", "dv", evv)
                if kind == 0:
                    dst = qt_s[jt, :, tc * 512:(tc + 1) * 512]
                elif kind == 1:
                    dst = kt_s[jt - 16, :, tc * 512:(tc + 1) * 512]
                else:
                    dst = vt_s[jt - 32, :, tc * 512:(tc + 1) * 512]
                sv = E.do("q", lambda q, dst=dst, st_i=st_i: q.dma_start(
                    out=dst, in_=p1_st[st_i][:]), "sst", 16)
                E.mark(("p1st", st_i), "sst", sv)
        if last_mm_of_group is not None:
            E.mark(("p1w", wbuf), "pe", last_mm_of_group)

    E.barrier()

    E.do("q", lambda q: q.dma_start(out=mask_sb[:], in_=masks[:]), "qld", 16)
    mask_ld = E.cnt["qld"]

    def fetch_head(h_, buf):
        E.do("g", lambda g, h_=h_, buf=buf: g.dma_start(
            out=p2_kt[buf][:], in_=kt_s[h_]), "gld", 16)
        E.do("g", lambda g, h_=h_, buf=buf: g.dma_start(
            out=p2_vt[buf][:], in_=vt_s[h_]), "gld", 16)
        E.gate("g", ("p2qt",))
        return E.do("g", lambda g, h_=h_: g.dma_start(
            out=p2_qt[0][:], in_=qt_s[h_]), "gld", 16)

    def attbase(g_, s_):
        return (s_ * 3072) if g_ % 2 == 0 else (12288 + s_ * 4096)

    for h_ in range(H):
        hb = h_ % 2
        E.gate("g", ("p2kt", hb))
        hv = fetch_head(h_, hb)

        E.w("t", "gld", hv)
        for q8 in range(4):
            bank = tbanks[q8 % 2]
            E.gate("t", ("ptb",))
            last = None
            for k in range(8):
                kt = q8 * 8 + k
                last = E.do("t", lambda t, bank=bank, k=k, kt=kt, b=hb: t.transpose(
                    bank[:, k * 128:(k + 1) * 128],
                    p2_vt[b][:, kt * 128:(kt + 1) * 128], ident[:]),
                    "pe" if k == 7 else None)
            E.w("v", "pe", last)
            E.gate("v", ("p2v",))
            ev = E.do("v", lambda v, bank=bank, q8=q8, b=hb: v.tensor_copy(
                p2_v[b][:, q8 * 1024:(q8 + 1) * 1024], bank[:]), "dv")
            E.mark(("ptb",), "dv", ev)
        vtr_done = E.cnt["dv"]

        ucnt = [0]
        gstate = {}

        def b_steps(g_, hb=hb, ucnt=ucnt):
            nu = g_ + 1
            steps = []

            def grp_head():
                if g_ > 0:
                    E.w("v", "ac", gstate[g_ - 1]["exp_done"])
                if g_ >= 2:
                    E.w("s", "pe", gstate[g_ - 2]["cd_pe"])
                    E.w("v", "pe", gstate[g_ - 2]["cd_pe"])
            steps.append(grp_head)

            def unit(s_, u, gi, ab):
                sbank = sunits[ucnt[0] % 3]
                sb_i = ucnt[0] % 3
                ss = p2_ss[ucnt[0] % 4]
                ss_i = ucnt[0] % 4
                ucnt[0] += 1
                E.gate("t", ("p2s", sb_i))
                mm = None
                for half in range(2):
                    mm = E.do("t", lambda t, sbank=sbank, b=hb, gi=gi, u=u, half=half:
                              t.matmul(sbank[:, half * 512:(half + 1) * 512],
                                       p2_qt[b][:, gi * 128:(gi + 1) * 128],
                                       p2_kt[b][:, u * 1024 + half * 512:
                                               u * 1024 + (half + 1) * 512],
                                       start=True, stop=True), "pe")
                E.w("s", "pe", mm)
                E.gate("s", ("p2ss", ss_i))
                cp = E.do("s", lambda s, sbank=sbank, ss=ss: s.copy(ss[:], sbank[:]), "ac")
                E.mark(("p2s", sb_i), "ac", cp)
                E.w("v", "ac", cp)
                if u == nu - 1:
                    E.w("v", "qld", mask_ld)
                    mv = E.do("v", lambda v, ss=ss, gi=gi: v.tensor_tensor(
                        ss[:], ss[:],
                        mask_sb[:, gi * 1024:(gi + 1) * 1024], ALU.add), "dv")
                    E.w("v", "dv", mv)
                r1 = E.do("v", lambda v, ss=ss, u=u: v.reduce_max(
                    mtmp[:, u:u + 1], ss[:], axis=AX.X, negate=True), "dv")
                E.w("s", "dv", r1)
                ev = E.do("s", lambda s, ss=ss, ab=ab, u=u: s.activation(
                    p2_att[:, ab + u * 1024: ab + (u + 1) * 1024], ss[:],
                    AF.Exp, bias=mtmp[:, u:u + 1],
                    accum_out=chsum[:, u:u + 1]), "ac")
                E.mark(("p2ss", ss_i), "ac", ev)

            def tail(s_, ab):
                exp_last = E.cnt["ac"]
                E.w("v", "ac", exp_last)
                t1 = E.do("v", lambda v: v.tensor_reduce(
                    rt_m[:], mtmp[:, 0:nu], AX.X, ALU.min), "dv")
                E.w("s", "dv", t1)
                f1 = E.do("s", lambda s: s.activation(
                    fixall[:, 0:nu], mtmp[:, 0:nu], AF.Exp,
                    bias=rt_m[:], scale=-1.0), "ac")
                E.w("v", "ac", f1)
                f2 = E.do("v", lambda v: v.tensor_mul(
                    fsum[:, 0:nu], chsum[:, 0:nu], fixall[:, 0:nu]), "dv")
                E.w("v", "dv", f2)
                f3 = E.do("v", lambda v: v.reduce_sum(
                    st_stot[:], fsum[:, 0:nu], axis=AX.X), "dv")
                E.w("v", "dv", f3)
                f4 = E.do("v", lambda v: v.reciprocal(st_rstd[:], st_stot[:]), "dv")
                E.w("v", "dv", f4)
                f5 = E.do("v", lambda v: v.tensor_scalar(
                    fsum[:, 0:nu], fixall[:, 0:nu], st_rstd[:], None, ALU.mult), "dv")
                E.w("v", "dv", f5)
                for u in range(nu):
                    E.do("v", lambda v, ab=ab, u=u: v.tensor_scalar(
                        p2_att[:, ab + u * 1024: ab + (u + 1) * 1024],
                        p2_att[:, ab + u * 1024: ab + (u + 1) * 1024],
                        fsum[:, u:u + 1], None, ALU.mult), "dv")

            for s_ in range(4):
                gi = 4 * g_ + s_
                ab = attbase(g_, s_)
                for u in range(nu):
                    steps.append(lambda s_=s_, u=u, gi=gi, ab=ab: unit(s_, u, gi, ab))
                steps.append(lambda s_=s_, ab=ab: tail(s_, ab))

            def grp_end():
                gstate[g_] = dict(att_done=E.cnt["dv"], exp_done=E.cnt["ac"])
            steps.append(grp_end)
            return steps

        def cd_steps(g_, hb=hb):
            nu = g_ + 1
            nkt = 8 * nu
            state = dict(ev_of_pair={}, av_last=None)

            def head():
                E.w("t", "dv", max(gstate[g_]["att_done"], vtr_done))
                E.w("t", "ac", gstate[g_]["exp_done"])

            def emit_C(p):
                bank = tbanks[p % 2]
                E.gate("t", ("ptb",))
                E.gate("t", ("p2attT", p))
                last = None
                for k2 in range(2):
                    kt = 2 * p + k2
                    for s_ in range(4):
                        ab = attbase(g_, s_)
                        last = E.do("t", lambda t, bank=bank, k2=k2, s_=s_, kt=kt, ab=ab:
                                    t.transpose(bank[:, k2 * 512 + s_ * 128:
                                                     k2 * 512 + (s_ + 1) * 128],
                                                p2_att[:, ab + kt * 128: ab + (kt + 1) * 128],
                                                ident[:]),
                                    "pe" if (k2 == 1 and s_ == 3) else None)
                eng = "v" if p % 2 == 0 else "s"
                E.w(eng, "pe", last)
                if eng == "v":
                    ev = E.do("v", lambda v, bank=bank, p=p: v.tensor_copy(
                        p2_attT[:, p * 1024:(p + 1) * 1024], bank[:]), "dv")
                    E.mark(("ptb",), "dv", ev)
                    state["ev_of_pair"][p] = ("dv", ev)
                else:
                    ev = E.do("s", lambda s, bank=bank, p=p: s.copy(
                        p2_attT[:, p * 1024:(p + 1) * 1024], bank[:]), "ac")
                    E.mark(("ptb",), "ac", ev)
                    state["ev_of_pair"][p] = ("ac", ev)

            def emit_D(kt):
                sem, val = state["ev_of_pair"][kt // 2]
                E.w("t", sem, val)
                if kt == 0:
                    E.gate("t", ("p2y",))
                state["av_last"] = E.do("t", lambda t, kt=kt, b=hb: t.matmul(
                    PY[:], p2_v[b][:, kt * 128:(kt + 1) * 128],
                    p2_attT[:, kt * 512:(kt + 1) * 512],
                    start=(kt == 0), stop=(kt == nkt - 1)),
                    "pe" if kt == nkt - 1 else None)
                E.mark(("p2attT", kt // 2), "pe", E.cnt["pe"])

            def yevac():
                yb = g_ % 2
                E.gate("v", ("p2yst", yb))
                E.w("v", "pe", state["av_last"])
                ye = E.do("v", lambda v, yb=yb: v.tensor_copy(p2_yst[yb][:], PY[:]), "dv")
                E.mark(("p2y",), "dv", ye)
                E.w("q", "dv", ye)
                yw = E.do("q", lambda q, h_=h_, g_=g_, yb=yb: q.dma_start(
                    out=yt_s[h_, 4 * g_:4 * g_ + 4].rearrange("tt hd q -> hd tt q"),
                    in_=p2_yst[yb][:].rearrange("p (tt q) -> p tt q", q=128)), "sst", 16)
                E.mark(("p2yst", yb), "sst", yw)
                gstate[g_]["cd_pe"] = E.cnt["pe"]

            steps = [head]
            npair = nkt // 2
            for p in range(npair):
                steps.append(lambda p=p: emit_C(p))
                if p >= 1:
                    steps.append(lambda p=p: (emit_D(2 * (p - 1)), emit_D(2 * p - 1)))
            steps.append(lambda: (emit_D(nkt - 2), emit_D(nkt - 1)))
            steps.append(yevac)
            return steps

        for st in b_steps(0):
            st()
        for g_ in range(4):
            bs = b_steps(g_ + 1) if g_ < 3 else []
            cs = cd_steps(g_)
            nb, ncd = len(bs), len(cs)
            bi = 0
            for ci, c in enumerate(cs):
                c()
                want = min(nb, (ci + 1) * 2 * nb // ncd)
                while bi < want:
                    bs[bi]()
                    bi += 1
            while bi < nb:
                bs[bi]()
                bi += 1
        E.mark(("p2kt", hb), "pe", E.cnt["pe"])
        E.mark(("p2qt",), "pe", E.cnt["pe"])
        E.mark(("p2v",), "pe", E.cnt["pe"])

    E.barrier()

    for i in range(4):
        E.do("g", lambda g, i=i: g.dma_start(
            out=p3_wp[:].rearrange("p (dt c) -> p dt c", c=D)[:, i * 4:(i + 1) * 4, :],
            in_=wproj[i * 4:(i + 1) * 4].rearrange("dt p c -> p dt c")), "gld", 16)
    wp_ld = E.cnt["gld"]
    E.do("q", lambda q: q.dma_start(out=p3_bp[:], in_=bproj.broadcast_to([128, D])), "qld", 16)
    bp_ld = E.cnt["qld"]

    for tt in range(NTT):
        buf = tt % 2
        E.gate("q", ("p3yt", buf))
        yv = E.do("q", lambda q, tt=tt, buf=buf: q.dma_start(
            out=p3_yt[buf][:].rearrange("p (ct q) -> p ct q", q=128),
            in_=yt_s[:, tt].rearrange("ct hd q -> hd ct q")), "qld", 16)
        E.gate("q", ("p3x", buf))
        xv = E.do("q", lambda q, tt=tt, buf=buf: q.dma_start(
            out=p3_x[buf][:], in_=x_own[tt * 128:(tt + 1) * 128, :]), "qld", 16)
        E.gate("v", ("p3x1", buf))
        last_e2 = None
        for jc in range(4):
            bank_i = jc % 4
            bank = fbanks[bank_i]
            E.gate("t", ("fb", bank_i))
            E.w("t", "qld", yv)
            E.w("t", "gld", wp_ld)
            last = None
            for ct in range(NDT):
                last = E.do("t", lambda t, bank=bank, buf=buf, ct=ct, jc=jc:
                            t.matmul(bank, p3_yt[buf][:, ct * 128:(ct + 1) * 128],
                                     p3_wp[:, ct * D + jc * 512: ct * D + (jc + 1) * 512],
                                     start=(ct == 0), stop=(ct == NDT - 1)),
                            "pe" if ct == NDT - 1 else None)
            E.w("v", "pe", last)
            E.w("v", "qld", max(bp_ld, xv))
            e1 = E.do("v", lambda v, bank=bank, buf=buf, jc=jc: v.tensor_add(
                p3_x1[buf][:, jc * 512:(jc + 1) * 512], bank,
                p3_bp[:, jc * 512:(jc + 1) * 512]), "dv")
            E.w("v", "dv", e1)
            e2 = E.do("v", lambda v, buf=buf, jc=jc: v.tensor_add(
                p3_x1[buf][:, jc * 512:(jc + 1) * 512],
                p3_x1[buf][:, jc * 512:(jc + 1) * 512],
                p3_x[buf][:, jc * 512:(jc + 1) * 512]), "dv")
            E.mark(("fb", bank_i), "dv", e2)
            last_e2 = e2
        E.mark(("p3yt", buf), "pe", E.cnt["pe"])
        E.mark(("p3x", buf), "dv", last_e2)
        E.w("q", "dv", last_e2)
        E.do("q", lambda q, tt=tt, buf=buf: q.dma_start(
            out=x1_s[tt], in_=p3_x1[buf][:]), "sst", 16)
        s2v = E.do("q", lambda q, tt=tt, buf=buf: q.dma_start(
            out=out[tt * 128:(tt + 1) * 128, :], in_=p3_x1[buf][:]), "sst", 16)
        E.mark(("p3x1", buf), "sst", s2v)

    E.barrier()

    for tt in range(NTT):
        buf = tt % 2
        E.gate("q", ("p4x", buf))
        xv = E.do("q", lambda q, tt=tt, buf=buf: q.dma_start(
            out=p4_x[buf][:], in_=x1_s[tt]), "qld", 16)
        E.gate("v", ("p4h", buf))
        vn = emit_ln_tile(p4_x[buf][:], "qld", xv, p4_h[buf][:], p4_junk[:])
        E.mark(("p4x", buf), "dv", vn)
        E.w("t", "dv", vn)
        ht_done = None
        for half in range(2):
            bank = tbanks[half]
            E.gate("t", ("ptb",))
            last = None
            for k in range(8):
                dt = half * 8 + k
                last = E.do("t", lambda t, buf=buf, dt=dt, bank=bank, k=k: t.transpose(
                    bank[:, k * 128:(k + 1) * 128],
                    p4_h[buf][:, dt * 128:(dt + 1) * 128], ident[:]),
                    "pe" if k == 7 else None)
            E.w("v", "pe", last)
            E.gate("v", ("p4ht", buf))
            E.gate("v", ("p4htS", buf))
            ev = E.do("v", lambda v, half=half, bank=bank, buf=buf: v.tensor_copy(
                p4_ht[buf][:, half * 1024:(half + 1) * 1024], bank[:]), "dv")
            E.mark(("ptb",), "dv", ev)
            if half == 1:
                E.mark(("p4h", buf), "dv", ev)
                ht_done = ev
        E.w("q", "dv", ht_done)
        hts = E.do("q", lambda q, tt=tt, buf=buf: q.dma_start(
            out=ht_s[:, :, tt * 128:(tt + 1) * 128].rearrange("dt d c -> d dt c"),
            in_=p4_ht[buf][:].rearrange("p (dt c) -> p dt c", c=128)), "sst", 16)
        E.mark(("p4htS", buf), "sst", hts)
        E.gate("t", ("p4pc",))
        E.w("t", "dv", ht_done)
        E.w("t", "gld", const_gld)
        last = None
        for dt in range(NDT):
            last = E.do("t", lambda t, buf=buf, dt=dt: t.matmul(
                PY[:, 0:NE], p4_ht[buf][:, dt * 128:(dt + 1) * 128],
                gwt_sb[:, dt * NE:(dt + 1) * NE],
                start=(dt == 0), stop=(dt == NDT - 1)),
                "pe" if dt == NDT - 1 else None)
        E.mark(("p4ht", buf), "pe", last)
        E.w("v", "pe", last)
        r1 = E.do("v", lambda v: v.reduce_max(rt_m[:], PY[:, 0:NE], axis=AX.X, negate=True), "dv")
        E.w("s", "dv", r1)
        r2 = E.do("s", lambda s: s.activation(rt_sm[:], PY[:, 0:NE], AF.Exp,
                                              bias=rt_m[:], accum_out=rt_s[:]), "ac")
        E.mark(("p4pc",), "ac", r2)
        E.w("v", "ac", r2)
        r3 = E.do("v", lambda v: v.reciprocal(rt_m[:], rt_s[:]), "dv")
        E.w("v", "dv", r3)
        r4 = E.do("v", lambda v, tt=tt: v.tensor_scalar(
            cw_sb[:, tt * NE:(tt + 1) * NE], rt_sm[:], rt_m[:], None, ALU.mult), "dv")
        E.w("v", "dv", r4)
        r5 = E.do("v", lambda v, tt=tt: v.tensor_copy(
            rt_wk[:], cw_sb[:, tt * NE:(tt + 1) * NE]), "dv")
        E.w("v", "dv", r5)
        for it in range(TOPK - 1):
            a1 = E.do("v", lambda v: v.reduce_max(rt_m[:], rt_wk[:], axis=AX.X), "dv")
            E.w("v", "dv", a1)
            a2 = E.do("v", lambda v: v.tensor_scalar(
                rt_ge[:], rt_wk[:], rt_m[:], None, ALU.is_ge), "dv")
            E.w("v", "dv", a2)
            a3 = E.do("v", lambda v: v.scalar_tensor_tensor(
                rt_wk[:], rt_ge[:], -2.0, rt_wk[:], ALU.mult, ALU.add), "dv")
            E.w("v", "dv", a3)
        b1 = E.do("v", lambda v: v.reduce_max(rt_m[:], rt_wk[:], axis=AX.X), "dv")
        E.w("v", "dv", b1)
        b2 = E.do("v", lambda v, tt=tt: v.tensor_scalar(
            rt_ge[:], cw_sb[:, tt * NE:(tt + 1) * NE], rt_m[:], None, ALU.is_ge), "dv")
        E.w("v", "dv", b2)
        E.do("v", lambda v, tt=tt: v.tensor_mul(
            cw_sb[:, tt * NE:(tt + 1) * NE], cw_sb[:, tt * NE:(tt + 1) * NE],
            rt_ge[:]), "dv")

    E.barrier()

    PANELS = [(0, 256), (256, 256), (512, 256), (768, 256), (1024, 256), (1280, 128)]
    POFF = [0, 4096, 8192, 12288, 16384, 20480]
    p3_out_stores = E.cnt["sst"]
    first_accum = True

    panel_uv = {}

    def emit_panel_pair(gi_, pi):
        ee = gi_ % NE
        m0, mw = PANELS[pi]
        off = POFF[pi]
        pb = pi % 2
        E.gate("g", ("p5wg", pb))
        E.do("g", lambda g, ee=ee, off=off, mw=mw, pb=pb: g.dma_start(
            out=p5_wg[pb][:, 0:NDT * mw],
            in_=wg_t[ee, :, off:off + NDT * mw]), "gld", 16)
        E.gate("g", ("p5wu", pb))
        panel_uv[(gi_, pi)] = E.do("g", lambda g, ee=ee, off=off, mw=mw, pb=pb: g.dma_start(
            out=p5_wu[pb][:, 0:NDT * mw],
            in_=wu_t[ee, :, off:off + NDT * mw]), "gld", 16)

    for ch in range(2):
        E.gate("q", ("p5ht",))
        hv5 = E.do("q", lambda q, ch=ch: q.dma_start(
            out=p5_ht[:].rearrange("p (dt c) -> p dt c", c=1024),
            in_=ht_s[:, :, ch * 1024:(ch + 1) * 1024].rearrange("dt d c -> d dt c")),
            "qld", 16)
        for e in range(NE):
            gidx = ch * NE + e
            E.gate("g", ("p5wd", 0))
            wd_v = {}
            wd_v[0] = E.do("g", lambda g, e=e: g.dma_start(
                out=p5_wd[0][:], in_=wd_t[e, :, 0:11264]), "gld", 16)
            zlast = {}
            for pi, (m0, mw) in enumerate(PANELS):
                pb = pi % 2
                if (gidx, pi) not in panel_uv:
                    emit_panel_pair(gidx, pi)
                uv = panel_uv[(gidx, pi)]
                if pi == 3:
                    E.gate("g", ("p5wd", 1))
                    wd_v[1] = E.do("g", lambda g, e=e: g.dma_start(
                        out=p5_wd[1][:], in_=wd_t[e, :, 11264:22528]), "gld", 16)
                for mm_j in range(mw // 128):
                    mt = (m0 + mm_j * 128) // 128
                    E.gate("t", ("p5pa",))
                    E.w("t", "gld", uv)
                    E.w("t", "qld", hv5)
                    lg = None
                    for dt in range(NDT):
                        wgs = p5_wg[pb][:, dt * mw + mm_j * 128: dt * mw + mm_j * 128 + 128]
                        for th in range(2):
                            lg = E.do("t", lambda t, wgs=wgs, dt=dt, th=th: t.matmul(
                                PA[:, th * 512:(th + 1) * 512], wgs,
                                p5_ht[:, dt * 1024 + th * 512: dt * 1024 + (th + 1) * 512],
                                start=(dt == 0), stop=(dt == NDT - 1)),
                                "pe" if (dt == NDT - 1 and th == 1) else None)
                    sgb = mt % 2
                    E.gate("s", ("p5sg", sgb))
                    E.w("s", "pe", lg)
                    sv = E.do("s", lambda s, sgb=sgb: s.activation(
                        p5_sg[sgb][:], PA[:], AF.Silu), "ac")
                    E.mark(("p5pa",), "ac", sv)
                    E.gate("t", ("p5pb",))
                    lu = None
                    for dt in range(NDT):
                        wus = p5_wu[pb][:, dt * mw + mm_j * 128: dt * mw + mm_j * 128 + 128]
                        for th in range(2):
                            lu = E.do("t", lambda t, wus=wus, dt=dt, th=th: t.matmul(
                                PB[:, th * 512:(th + 1) * 512], wus,
                                p5_ht[:, dt * 1024 + th * 512: dt * 1024 + (th + 1) * 512],
                                start=(dt == 0), stop=(dt == NDT - 1)),
                                "pe" if (dt == NDT - 1 and th == 1) else None)
                    E.gate("v", ("p5z", mt))
                    E.w("v", "ac", sv)
                    E.w("v", "pe", lu)
                    zv = E.do("v", lambda v, sgb=sgb, mt=mt: v.tensor_mul(
                        p5_z[:, mt * 1024:(mt + 1) * 1024], p5_sg[sgb][:], PB[:]), "dv")
                    E.mark(("p5pb",), "dv", zv)
                    E.mark(("p5sg", sgb), "dv", zv)
                    zlast[mt] = zv
                E.mark(("p5wg", pb), "pe", E.cnt["pe"])
                E.mark(("p5wu", pb), "pe", E.cnt["pe"])

            if gidx + 1 < 2 * NE:
                emit_panel_pair(gidx + 1, 0)
                emit_panel_pair(gidx + 1, 1)

            bank_sets = [(PA[:, 0:512], PA[:, 512:1024], ("p5pa",)),
                         (PB[:, 0:512], PB[:, 512:1024], ("p5pb",)),
                         (PS3[:, 0:512], PS3[:, 512:1024], ("p5pcd",))]
            for jh in range(2):
                wb = jh
                dv5 = wd_v[jh]
                for ts in range(8):
                    tt = ch * 8 + ts
                    bs = (jh * 8 + ts) % 3
                    bk0, bk1, bres = bank_sets[bs]
                    E.gate("t", bres)
                    E.w("t", "gld", dv5)
                    l1 = None
                    for mt in range(MT):
                        E.w("t", "dv", zlast[mt])
                        E.do("t", lambda t, mt=mt, ts=ts, bk0=bk0, wb=wb: t.matmul(
                            bk0, p5_z[:, mt * 1024 + ts * 128: mt * 1024 + ts * 128 + 128],
                            p5_wd[wb][:, mt * 1024: mt * 1024 + 512],
                            start=(mt == 0), stop=(mt == MT - 1)),
                            None)
                        l1 = E.do("t", lambda t, mt=mt, ts=ts, bk1=bk1, wb=wb: t.matmul(
                            bk1, p5_z[:, mt * 1024 + ts * 128: mt * 1024 + ts * 128 + 128],
                            p5_wd[wb][:, mt * 1024 + 512: mt * 1024 + 1024],
                            start=(mt == 0), stop=(mt == MT - 1)),
                            "pe" if mt == MT - 1 else None)
                    evb = (jh * 8 + ts) % 3
                    E.gate("v", ("p5ev", evb))
                    E.w("v", "pe", l1)
                    cwcol = cw_sb[:, tt * NE + e: tt * NE + e + 1]
                    E.do("v", lambda v, evb=evb, bk0=bk0, cwcol=cwcol: v.tensor_scalar(
                        p5_ev[evb][:, 0:512], bk0, cwcol, None, ALU.mult), "dv")
                    x2v = E.do("v", lambda v, evb=evb, bk1=bk1, cwcol=cwcol: v.tensor_scalar(
                        p5_ev[evb][:, 512:1024], bk1, cwcol, None, ALU.mult), "dv")
                    E.mark(bres, "dv", x2v)
                    E.w("g", "dv", x2v)
                    if first_accum:
                        E.w("g", "sst", p3_out_stores)
                        first_accum = False
                    av = E.do("g", lambda g, tt=tt, jh=jh, evb=evb: g.dma_start(
                        out=out[tt * 128:(tt + 1) * 128, jh * 1024:(jh + 1) * 1024],
                        in_=p5_ev[evb][:], accum_op=ALU.add), "gst", 16)
                    E.mark(("p5ev", evb), "gst", av)
                E.mark(("p5wd", wb), "pe", E.cnt["pe"])
            for mt in range(MT):
                E.mark(("p5z", mt), "pe", E.cnt["pe"])
        E.mark(("p5ht",), "pe", E.cnt["pe"])

    E.barrier()

    with nc.Block() as blk:
        blk.tensor(lambda t: [f(t) for f in E.ops["t"]])
        blk.vector(lambda v: [f(v) for f in E.ops["v"]])
        blk.scalar(lambda s: [f(s) for f in E.ops["s"]])
        blk.sync(lambda q: [f(q) for f in E.ops["q"]])
        blk.gpsimd(lambda g: [f(g) for f in E.ops["g"]])
    return nc


def make_masks(h_par):
    m = np.zeros((128, NTT * 1024), dtype=np.float32)
    r = np.arange(128)[:, None]
    c = np.arange(1024)[None, :]
    for i in range(NTT):
        jq = 2 * i + h_par
        qg = jq * 128 + r
        ub = i // 4
        kpos = ub * 1024 + c
        pb = kpos // 128
        ob = (pb ^ 1) if h_par == 1 else pb
        korig = ob * 128 + kpos % 128
        vis = korig <= qg
        m[:, i * 1024:(i + 1) * 1024] = np.where(vis, 0.0, -1e30).astype(np.float32)
    return m.astype(BF)


def kernel(L0_attn_x, L0_attn_w_qkv, L0_attn_b_qkv, L0_attn_w_proj, L0_attn_b_proj,
           L0_ffn_gate_w, L0_ffn_w_gate, L0_ffn_w_up, L0_ffn_w_down):
    x = np.ascontiguousarray(np.asarray(L0_attn_x, dtype=np.float32))
    wqkv = np.ascontiguousarray(
        np.asarray(L0_attn_w_qkv, dtype=np.float32).reshape(NDT, 128, 24, 256)
        .transpose(1, 2, 0, 3).reshape(128, 24 * NDT * 256))
    bqkv = np.ascontiguousarray(np.asarray(L0_attn_b_qkv, dtype=np.float32).reshape(48, 128).T)
    wproj = np.ascontiguousarray(np.asarray(L0_attn_w_proj, dtype=np.float32)).reshape(NDT, 128, D)
    bproj = np.ascontiguousarray(np.asarray(L0_attn_b_proj, dtype=np.float32).reshape(1, D))
    gate_w = np.asarray(L0_ffn_gate_w, dtype=np.float32)
    gwt = np.ascontiguousarray(
        gate_w.T.reshape(NDT, 128, NE).transpose(1, 0, 2).reshape(128, NDT * NE))
    PANELS = [(0, 256), (256, 256), (512, 256), (768, 256), (1024, 256), (1280, 128)]
    def pack_gu(w):
        w3 = np.asarray(w, dtype=np.float32).transpose(0, 2, 1).reshape(NE, NDT, 128, MD)
        return np.ascontiguousarray(np.concatenate(
            [w3[:, :, :, m0:m0 + pw].transpose(0, 2, 1, 3).reshape(NE, 128, NDT * pw)
             for m0, pw in PANELS], axis=2))
    wg_t = pack_gu(L0_ffn_w_gate)
    wu_t = pack_gu(L0_ffn_w_up)
    wd3 = np.asarray(L0_ffn_w_down, dtype=np.float32).transpose(0, 2, 1).reshape(NE, MT, 128, D)
    wd_t = np.ascontiguousarray(np.concatenate(
        [wd3[:, :, :, jh * 1024:(jh + 1) * 1024].transpose(0, 2, 1, 3).reshape(NE, 128, MT * 1024)
         for jh in range(2)], axis=2))
    ident = np.eye(128, dtype=BF)

    nc = build_program()

    swap = np.arange(T // 128).reshape(-1, 2)[:, ::-1].reshape(-1)
    in_maps = []
    for core in range(8):
        b, h_par = core // 2, core % 2
        xb = x[b]
        if h_par == 1:
            x_perm = np.ascontiguousarray(xb.reshape(T // 128, 128, D)[swap].reshape(T, D))
        else:
            x_perm = xb
        x_o = np.ascontiguousarray(
            xb.reshape(T // 128, 128, D)[h_par::2].reshape(OWN, D))
        in_maps.append({
            "x_b": x_perm, "x_own": x_o,
            "wqkv": wqkv, "bqkv": bqkv, "wproj": wproj, "bproj": bproj,
            "gwt": gwt, "wg_t": wg_t, "wu_t": wu_t, "wd_t": wd_t,
            "masks": make_masks(h_par), "ident": ident,
        })
    res = run_bass_kernel_spmd(nc, in_maps, core_ids=list(range(8)))

    full = np.empty((B, T, D), dtype=np.float32)
    for core in range(8):
        b, h_par = core // 2, core % 2
        o = np.asarray(res.results[core]["out"])
        for tt in range(NTT):
            full[b, (2 * tt + h_par) * 128:(2 * tt + h_par + 1) * 128, :] = \
                o[tt * 128:(tt + 1) * 128, :]
    return full


# version 15
# speedup vs baseline: 1.0401x; 1.0401x over previous
import math
import numpy as np
import ml_dtypes
import concourse.bass as bass
import concourse.mybir as mybir
from concourse.bass_utils import run_bass_kernel_spmd

F32 = mybir.dt.float32
BF16 = mybir.dt.bfloat16
AF = mybir.ActivationFunctionType
ALU = mybir.AluOpType
AX = mybir.AxisListType

B, T, D = 4, 4096, 2048
H, HD = 16, 128
NE, MD = 16, 1408
MT = MD // 128
TOPK = 6
ATTN_MULT = 1.0 / math.sqrt(HD)
NEG_BIG = -1e30 / ATTN_MULT
EPS = 1e-5
OWN = 2048
NTT = OWN // 128
NDT = D // 128

BF = ml_dtypes.bfloat16


class Emitter:
    ENGINES = ("t", "v", "s", "q", "g")

    def __init__(self, nc, sem_names):
        self.nc = nc
        self.sems = {n: nc.alloc_semaphore(n) for n in sem_names}
        self.cnt = {n: 0 for n in sem_names}
        self.ops = {e: [] for e in self.ENGINES}
        self.free = {}

    def w(self, e, sem, val=None):
        v = self.cnt[sem] if val is None else val
        if v is None or v <= 0:
            return
        h = self.sems[sem]
        self.ops[e].append(lambda eng, h=h, v=v: eng.wait_ge(h, v))

    def do(self, e, fn, inc=None, n=1):
        if inc is None:
            self.ops[e].append(fn)
            return None
        self.cnt[inc] += n
        h = self.sems[inc]
        v = self.cnt[inc]
        self.ops[e].append(lambda eng, fn=fn, h=h, n=n: fn(eng).then_inc(h, n))
        return v

    def mark(self, res, sem, val):
        self.free[res] = (sem, val)

    def gate(self, e, res):
        if res in self.free:
            sem, val = self.free[res]
            self.w(e, sem, val)

    def begin_defer(self, e):
        self._saved = self.ops[e]
        self.ops[e] = []

    def end_defer(self, e):
        cap = self.ops[e]
        self.ops[e] = self._saved
        return cap

    def barrier(self):
        snapshot = [(n, self.cnt[n]) for n in self.sems if self.cnt[n] > 0]
        for e in self.ENGINES:
            for n, v in snapshot:
                self.w(e, n, v)


def build_program():
    nc = bass.Bass("TRN2", target_bir_lowering=False)

    x_in = nc.dram_tensor("x_b", [T, D], F32, kind="ExternalInput").ap()
    x_own = nc.dram_tensor("x_own", [OWN, D], F32, kind="ExternalInput").ap()
    wqkv = nc.dram_tensor("wqkv", [128, 24 * NDT * 256], F32, kind="ExternalInput").ap()
    bqkv = nc.dram_tensor("bqkv", [128, 48], F32, kind="ExternalInput").ap()
    wproj = nc.dram_tensor("wproj", [NDT, 128, D], F32, kind="ExternalInput").ap()
    bproj = nc.dram_tensor("bproj", [1, D], F32, kind="ExternalInput").ap()
    gwt = nc.dram_tensor("gwt", [128, NDT * NE], F32, kind="ExternalInput").ap()
    wg_t = nc.dram_tensor("wg_t", [NE, 128, NDT * MD], F32, kind="ExternalInput").ap()
    wu_t = nc.dram_tensor("wu_t", [NE, 128, NDT * MD], F32, kind="ExternalInput").ap()
    wd_t = nc.dram_tensor("wd_t", [NE, 128, 2 * MT * 1024], F32, kind="ExternalInput").ap()
    masks = nc.dram_tensor("masks", [128, NTT * 1024], BF16, kind="ExternalInput").ap()
    ident_in = nc.dram_tensor("ident", [128, 128], BF16, kind="ExternalInput").ap()
    out = nc.dram_tensor("out", [OWN, D], F32, kind="ExternalOutput").ap()

    qt_s = nc.dram_tensor("qt_s", [H, 128, OWN], BF16, kind="Internal").ap()
    kt_s = nc.dram_tensor("kt_s", [H, 128, T], BF16, kind="Internal").ap()
    vt_s = nc.dram_tensor("vt_s", [H, 128, T], BF16, kind="Internal").ap()
    x1_s = nc.dram_tensor("x1_s", [NTT, 128, D], F32, kind="Internal").ap()
    ht_s = nc.dram_tensor("ht_s", [NDT, 128, OWN], BF16, kind="Internal").ap()
    yt_s = nc.dram_tensor("yt_s", [H, NTT, 128, 128], BF16, kind="Internal").ap()

    SBASE = 16512
    def sbt(name, shape, dtype, off):
        return nc.alloc_sbuf_tensor_at(name, shape, dtype, offset=SBASE + off).ap()

    ident = sbt("identb", [128, 128], BF16, 0)
    bq_sb = sbt("bq_sb", [128, 48], F32, 256)
    gwt_sb = sbt("gwt_sb", [128, NDT * NE], BF16, 448)
    cw_sb = sbt("cw_sb", [128, NTT * NE], F32, 960)
    recip_sb = sbt("recip_sb", [128, NTT], F32, 1984)
    st_sums = sbt("st_sums", [128, 1], F32, 2048)
    st_ssq = sbt("st_ssq", [128, 1], F32, 2080)
    st_nm = sbt("st_nm", [128, 1], F32, 2112)
    st_var = sbt("st_var", [128, 1], F32, 2144)
    st_msq = sbt("st_msq", [128, 1], F32, 2176)
    st_lnv = sbt("st_lnv", [128, 1], F32, 2208)
    st_rstd = sbt("st_rstd", [128, 1], F32, 2240)
    st_stot = sbt("st_stot", [128, 1], F32, 2272)
    mtmp = sbt("mtmp", [128, 8], F32, 2304)
    rall = sbt("rall", [128, 8], F32, 2336)
    negr = sbt("negr", [128, 8], F32, 2368)
    chsum = sbt("chsum", [128, 8], F32, 2400)
    fixall = sbt("fixall", [128, 8], F32, 2432)
    fsum = sbt("fsum", [128, 8], F32, 2464)
    rt_sm = sbt("rt_sm", [128, NE], F32, 2496)
    rt_wk = sbt("rt_wk", [128, NE], F32, 2560)
    rt_ge = sbt("rt_ge", [128, NE], F32, 2624)
    rt_m = sbt("rt_m", [128, 1], F32, 2688)
    rt_s = sbt("rt_s", [128, 1], F32, 2720)
    A0 = 4096

    ln1T = sbt("ln1T", [128, NDT * T], BF16, A0)
    p1_x = [sbt(f"p1x{i}", [128, D], F32, A0 + 131072 + i * 8192) for i in range(2)]
    p1_ln = [sbt(f"p1ln{i}", [128, D], BF16, A0 + 147456 + i * 4096) for i in range(2)]
    p1_junk = sbt("p1junk", [128, D], BF16, A0 + 155648)
    p1_w = [sbt(f"p1w{i}", [128, NDT * 256], BF16, A0 + 159744 + i * 8192) for i in range(2)]
    p1_st = [sbt(f"p1st{i}", [128, 512], BF16, A0 + 176128 + i * 1024) for i in range(4)]

    mask_sb = sbt("mask_sb", [128, NTT * 1024], BF16, A0)
    p2_kt = [sbt(f"p2kt{i}", [128, T], BF16, A0 + 65536 + i * 8192) for i in range(2)]
    p2_vt = [sbt(f"p2vt{i}", [128, T], BF16, A0 + 81920 + i * 8192) for i in range(2)]
    _p2qt = sbt("p2qt0", [128, OWN], BF16, A0 + 98304)
    p2_qt = [_p2qt, _p2qt]
    _p2v = sbt("p2v0", [128, T], BF16, A0 + 106496)
    p2_v = [_p2v, _p2v]
    p2_att = sbt("p2att", [128, 28672], BF16, A0 + 114688)
    p2_attT = sbt("p2attT", [128, 32 * 512], BF16, A0 + 172032)
    p2_yst = [sbt(f"p2yst{i}", [128, 512], BF16, A0 + 204800 + i * 1024) for i in range(2)]

    p3_wp = sbt("p3wp", [128, NDT * D], BF16, A0)
    p3_bp = sbt("p3bp", [128, D], F32, A0 + 65536)
    p3_yt = [sbt(f"p3yt{i}", [128, H * 128], BF16, A0 + 73728 + i * 4096) for i in range(2)]
    p3_x = [sbt(f"p3x{i}", [128, D], F32, A0 + 81920 + i * 8192) for i in range(2)]
    p3_x1 = [sbt(f"p3x1{i}", [128, D], F32, A0 + 98304 + i * 8192) for i in range(2)]

    p4_x = [sbt(f"p4x{i}", [128, D], F32, A0 + 155648 + i * 8192) for i in range(2)]
    p4_h = [sbt(f"p4h{i}", [128, D], BF16, A0 + 172032 + i * 4096) for i in range(2)]
    p4_ht = [sbt(f"p4ht{i}", [128, NDT * 128], BF16, A0 + 180224 + i * 4096) for i in range(2)]
    p4_junk = sbt("p4junk", [128, D], BF16, A0 + 188416)

    p5_ht = sbt("p5ht", [128, NDT * 1024], BF16, A0)
    p5_z = sbt("p5z", [128, MT * 1024], BF16, A0 + 32768)
    p5_wg = [sbt(f"p5wg{i}", [128, NDT * 256], BF16, A0 + 55296 + i * 8192) for i in range(2)]
    p5_wu = [sbt(f"p5wu{i}", [128, NDT * 256], BF16, A0 + 71680 + i * 8192) for i in range(2)]
    p5_wd = [sbt(f"p5wd{i}", [128, MT * 1024], BF16, A0 + 88064 + i * 22528) for i in range(2)]
    p5_sg = [sbt(f"p5sg{i}", [128, 1024], BF16, A0 + 133120 + i * 2048) for i in range(2)]
    p5_ev = [sbt(f"p5ev{i}", [128, 1024], F32, A0 + 137216 + i * 4096) for i in range(3)]

    PA = nc.alloc_psum_tensor("PA", [128, 1024], F32).ap()
    PB = nc.alloc_psum_tensor("PB", [128, 1024], F32).ap()
    PS3 = nc.alloc_psum_tensor("PS3", [128, 1024], F32).ap()
    PY = nc.alloc_psum_tensor("PY", [128, 512], F32).ap()
    PTA = nc.alloc_psum_tensor("PTA", [128, 1024], BF16).ap()
    fbanks = [PA[:, 0:512], PA[:, 512:1024], PB[:, 0:512], PB[:, 512:1024]]
    tbanks = [PTA, PTA]
    sunits = [PA, PB, PS3]

    E = Emitter(nc, ["gld", "qld", "sst", "gst", "pe", "dv", "ac", "ast"])

    E.do("q", lambda q: q.dma_start(out=ident[:], in_=ident_in[:]), "qld", 16)
    E.do("q", lambda q: q.dma_start(out=bq_sb[:], in_=bqkv[:]), "qld", 16)
    E.do("g", lambda g: g.dma_start(out=gwt_sb[:], in_=gwt[:]), "gld", 16)
    const_gld = E.cnt["gld"]

    def emit_ln_tile(xap, xload_sem, xload_val, lnout, junk):
        E.w("v", xload_sem, xload_val)
        E.do("v", lambda v, xap=xap: v.reduce_sum(st_sums[:], xap, axis=AX.X), "dv")
        v1 = E.do("v", lambda v, xap=xap, junk=junk: v.scalar_tensor_tensor(
            junk, xap, 1.0, xap, ALU.mult, ALU.mult, accum_out=st_ssq[:]), "dv")
        E.w("v", "dv", v1)
        E.do("v", lambda v: v.tensor_scalar(st_nm[:], st_sums[:], -1.0 / D, None, ALU.mult), "dv")
        v2 = E.do("v", lambda v: v.tensor_scalar(
            st_var[:], st_ssq[:], 1.0 / D, EPS, ALU.mult, ALU.add), "dv")
        E.w("v", "dv", v2)
        v3 = E.do("v", lambda v: v.scalar_tensor_tensor(
            st_msq[:], st_nm[:], 1.0, st_nm[:], ALU.mult, ALU.mult), "dv")
        E.w("v", "dv", v3)
        v4 = E.do("v", lambda v: v.tensor_sub(st_var[:], st_var[:], st_msq[:]), "dv")
        E.w("s", "dv", v4)
        s1 = E.do("s", lambda s: s.activation(st_lnv[:], st_var[:], AF.Ln), "ac")
        E.w("s", "ac", s1)
        s2 = E.do("s", lambda s: s.activation(st_rstd[:], st_lnv[:], AF.Exp, scale=-0.5), "ac")
        E.w("v", "ac", s2)
        vn = E.do("v", lambda v, xap=xap, lnout=lnout: v.tensor_scalar(
            lnout, xap, st_nm[:], st_rstd[:], ALU.add, ALU.mult), "dv")
        return vn

    lnv = {}
    for tt in range(T // 128):
        buf = tt % 2
        E.gate("q", ("p1x", buf))
        xv = E.do("q", lambda q, tt=tt, buf=buf: q.dma_start(
            out=p1_x[buf][:], in_=x_in[tt * 128:(tt + 1) * 128, :]), "qld", 16)
        E.gate("v", ("p1ln", buf))
        vn = emit_ln_tile(p1_x[buf][:], "qld", xv, p1_ln[buf][:], p1_junk[:])
        E.mark(("p1x", buf), "dv", vn)
        E.w("t", "dv", vn)
        E.w("t", "qld", 16)
        for half in range(2):
            bank = tbanks[half]
            E.gate("t", ("ptb",))
            last = None
            for k in range(8):
                dt = half * 8 + k
                last = E.do("t", lambda t, buf=buf, dt=dt, bank=bank, k=k: t.transpose(
                    bank[:, k * 128:(k + 1) * 128],
                    p1_ln[buf][:, dt * 128:(dt + 1) * 128], ident[:]),
                    "pe" if k == 7 else None)
            E.w("v", "pe", last)
            ev = E.do("v", lambda v, half=half, bank=bank, tt=tt: v.tensor_copy(
                ln1T.rearrange("p (dt c) -> p dt c", c=T)[:, half * 8:(half + 1) * 8,
                                                         tt * 128:(tt + 1) * 128],
                bank.rearrange("p (dt c) -> p dt c", c=128)), "dv")
            E.mark(("ptb",), "dv", ev)
            if half == 1:
                E.mark(("p1ln", buf), "dv", ev)
                lnv[tt] = ev

    E.w("v", "qld", 32)
    E.w("s", "qld", 32)
    ln3 = ln1T.rearrange("p (dt i c) -> p dt i c", i=T // 128, c=128)
    evac_eng = 0
    for grp in range(24):
        wbuf = grp % 2
        E.gate("g", ("p1w", wbuf))
        wv = E.do("g", lambda g, grp=grp, wbuf=wbuf: g.dma_start(
            out=p1_w[wbuf][:],
            in_=wqkv[:, grp * 4096:(grp + 1) * 4096]), "gld", 16)
        last_mm_of_group = None
        for jj in range(2):
            jt = grp * 2 + jj
            kind = jt // 16
            ntc = 4 if kind == 0 else 8
            for tc in range(ntc):
                bank_i = (jt * 8 + tc) % 4
                bank = fbanks[bank_i]
                E.gate("t", ("fb", bank_i))
                E.w("t", "gld", wv)
                need_tt = (8 * tc + 7) if kind == 0 else (4 * tc + 3)
                E.w("t", "dv", lnv[need_tt])
                last = None
                for dt in range(NDT):
                    if kind == 0:
                        rhs = ln3[:, dt, 8 * tc: 8 * tc + 8: 2, :]
                    else:
                        rhs = ln1T[:, dt * T + tc * 512: dt * T + (tc + 1) * 512]
                    last = E.do("t", lambda t, bank=bank, wbuf=wbuf, dt=dt, jj=jj,
                                rhs=rhs, first=(dt == 0), fin=(dt == NDT - 1):
                                t.matmul(bank, p1_w[wbuf][:, dt * 256 + jj * 128:
                                                          dt * 256 + jj * 128 + 128],
                                         rhs, start=first, stop=fin),
                                "pe" if dt == NDT - 1 else None)
                last_mm_of_group = last
                st_i = (jt * 8 + tc) % 4
                if kind == 0:
                    eng = "v"
                else:
                    eng = "s" if evac_eng % 2 == 0 else "v"
                    evac_eng += 1
                E.gate(eng, ("p1st", st_i))
                E.w(eng, "pe", last)
                if kind == 0:
                    evv = E.do("v", lambda v, st_i=st_i, bank=bank, jt=jt: v.tensor_scalar(
                        p1_st[st_i][:], bank, bq_sb[:, jt:jt + 1], ATTN_MULT,
                        ALU.add, ALU.mult), "dv")
                    E.mark(("fb", bank_i), "dv", evv)
                    E.w("q", "dv", evv)
                elif eng == "s":
                    evv = E.do("s", lambda s, st_i=st_i, bank=bank, jt=jt: s.add(
                        p1_st[st_i][:], bank, bq_sb[:, jt:jt + 1]), "ac")
                    E.mark(("fb", bank_i), "ac", evv)
                    E.w("q", "ac", evv)
                else:
                    evv = E.do("v", lambda v, st_i=st_i, bank=bank, jt=jt: v.tensor_scalar(
                        p1_st[st_i][:], bank, bq_sb[:, jt:jt + 1], None, ALU.add), "dv")
                    E.mark(("fb", bank_i), "dv", evv)
                    E.w("q", "dv", evv)
                if kind == 0:
                    dst = qt_s[jt, :, tc * 512:(tc + 1) * 512]
                elif kind == 1:
                    dst = kt_s[jt - 16, :, tc * 512:(tc + 1) * 512]
                else:
                    dst = vt_s[jt - 32, :, tc * 512:(tc + 1) * 512]
                sv = E.do("q", lambda q, dst=dst, st_i=st_i: q.dma_start(
                    out=dst, in_=p1_st[st_i][:]), "sst", 16)
                E.mark(("p1st", st_i), "sst", sv)
        if last_mm_of_group is not None:
            E.mark(("p1w", wbuf), "pe", last_mm_of_group)

    E.barrier()

    E.do("q", lambda q: q.dma_start(out=mask_sb[:], in_=masks[:]), "qld", 16)
    mask_ld = E.cnt["qld"]

    def fetch_head(h_, buf):
        E.do("g", lambda g, h_=h_, buf=buf: g.dma_start(
            out=p2_kt[buf][:], in_=kt_s[h_]), "gld", 16)
        E.do("g", lambda g, h_=h_, buf=buf: g.dma_start(
            out=p2_vt[buf][:], in_=vt_s[h_]), "gld", 16)
        E.gate("g", ("p2qt",))
        return E.do("g", lambda g, h_=h_: g.dma_start(
            out=p2_qt[0][:], in_=qt_s[h_]), "gld", 16)

    def attbase(g_, s_):
        return (s_ * 3072) if g_ % 2 == 0 else (12288 + s_ * 4096)

    for h_ in range(H):
        hb = h_ % 2
        E.gate("g", ("p2kt", hb))
        hv = fetch_head(h_, hb)

        E.w("t", "gld", hv)
        for q8 in range(4):
            bank = tbanks[q8 % 2]
            E.gate("t", ("ptb",))
            last = None
            for k in range(8):
                kt = q8 * 8 + k
                last = E.do("t", lambda t, bank=bank, k=k, kt=kt, b=hb: t.transpose(
                    bank[:, k * 128:(k + 1) * 128],
                    p2_vt[b][:, kt * 128:(kt + 1) * 128], ident[:]),
                    "pe" if k == 7 else None)
            E.w("v", "pe", last)
            E.gate("v", ("p2v",))
            ev = E.do("v", lambda v, bank=bank, q8=q8, b=hb: v.tensor_copy(
                p2_v[b][:, q8 * 1024:(q8 + 1) * 1024], bank[:]), "dv")
            E.mark(("ptb",), "dv", ev)
        vtr_done = E.cnt["dv"]

        ucnt = [0]
        gstate = {}

        def b_steps(g_, hb=hb, ucnt=ucnt):
            nu = g_ + 1
            steps = []

            def grp_head():
                if g_ > 0:
                    E.w("v", "ac", gstate[g_ - 1]["exp_done"])
                if g_ >= 2:
                    E.w("s", "pe", gstate[g_ - 2]["cd_pe"])
                    E.w("v", "pe", gstate[g_ - 2]["cd_pe"])
            steps.append(grp_head)

            def unit(s_, u, gi, ab):
                sbank = sunits[ucnt[0] % 3]
                sb_i = ucnt[0] % 3
                ucnt[0] += 1
                E.gate("t", ("p2s", sb_i))
                mm = None
                for half in range(2):
                    mm = E.do("t", lambda t, sbank=sbank, b=hb, gi=gi, u=u, half=half:
                              t.matmul(sbank[:, half * 512:(half + 1) * 512],
                                       p2_qt[b][:, gi * 128:(gi + 1) * 128],
                                       p2_kt[b][:, u * 1024 + half * 512:
                                               u * 1024 + (half + 1) * 512],
                                       start=True, stop=True), "pe")
                E.w("v", "pe", mm)
                if u == nu - 1:
                    E.w("v", "qld", mask_ld)
                    mv = E.do("v", lambda v, sbank=sbank, gi=gi: v.tensor_tensor(
                        sbank[:], sbank[:],
                        mask_sb[:, gi * 1024:(gi + 1) * 1024], ALU.add), "dv")
                    E.w("v", "dv", mv)
                r1 = E.do("v", lambda v, sbank=sbank, u=u: v.reduce_max(
                    mtmp[:, u:u + 1], sbank[:], axis=AX.X, negate=True), "dv")
                E.w("s", "dv", r1)
                ev = E.do("s", lambda s, sbank=sbank, ab=ab, u=u: s.activation(
                    p2_att[:, ab + u * 1024: ab + (u + 1) * 1024], sbank[:],
                    AF.Exp, bias=mtmp[:, u:u + 1],
                    accum_out=chsum[:, u:u + 1]), "ac")
                E.mark(("p2s", sb_i), "ac", ev)

            def tail(s_, ab):
                exp_last = E.cnt["ac"]
                E.w("v", "ac", exp_last)
                t1 = E.do("v", lambda v: v.tensor_reduce(
                    rt_m[:], mtmp[:, 0:nu], AX.X, ALU.min), "dv")
                E.w("s", "dv", t1)
                f1 = E.do("s", lambda s: s.activation(
                    fixall[:, 0:nu], mtmp[:, 0:nu], AF.Exp,
                    bias=rt_m[:], scale=-1.0), "ac")
                E.w("v", "ac", f1)
                f2 = E.do("v", lambda v: v.tensor_mul(
                    fsum[:, 0:nu], chsum[:, 0:nu], fixall[:, 0:nu]), "dv")
                E.w("v", "dv", f2)
                f3 = E.do("v", lambda v: v.reduce_sum(
                    st_stot[:], fsum[:, 0:nu], axis=AX.X), "dv")
                E.w("v", "dv", f3)
                f4 = E.do("v", lambda v: v.reciprocal(st_rstd[:], st_stot[:]), "dv")
                E.w("v", "dv", f4)
                f5 = E.do("v", lambda v: v.tensor_scalar(
                    fsum[:, 0:nu], fixall[:, 0:nu], st_rstd[:], None, ALU.mult), "dv")
                E.w("v", "dv", f5)
                for u in range(nu):
                    E.do("v", lambda v, ab=ab, u=u: v.tensor_scalar(
                        p2_att[:, ab + u * 1024: ab + (u + 1) * 1024],
                        p2_att[:, ab + u * 1024: ab + (u + 1) * 1024],
                        fsum[:, u:u + 1], None, ALU.mult), "dv")

            for s_ in range(4):
                gi = 4 * g_ + s_
                ab = attbase(g_, s_)
                for u in range(nu):
                    steps.append(lambda s_=s_, u=u, gi=gi, ab=ab: unit(s_, u, gi, ab))
                steps.append(lambda s_=s_, ab=ab: tail(s_, ab))

            def grp_end():
                gstate[g_] = dict(att_done=E.cnt["dv"], exp_done=E.cnt["ac"])
            steps.append(grp_end)
            return steps

        def cd_steps(g_, hb=hb):
            nu = g_ + 1
            nkt = 8 * nu
            state = dict(ev_of_pair={}, av_last=None)

            def head():
                E.w("t", "dv", max(gstate[g_]["att_done"], vtr_done))
                E.w("t", "ac", gstate[g_]["exp_done"])

            def emit_C(p):
                bank = tbanks[p % 2]
                E.gate("t", ("ptb",))
                E.gate("t", ("p2attT", p))
                last = None
                for k2 in range(2):
                    kt = 2 * p + k2
                    for s_ in range(4):
                        ab = attbase(g_, s_)
                        last = E.do("t", lambda t, bank=bank, k2=k2, s_=s_, kt=kt, ab=ab:
                                    t.transpose(bank[:, k2 * 512 + s_ * 128:
                                                     k2 * 512 + (s_ + 1) * 128],
                                                p2_att[:, ab + kt * 128: ab + (kt + 1) * 128],
                                                ident[:]),
                                    "pe" if (k2 == 1 and s_ == 3) else None)
                eng = "v" if p % 2 == 0 else "s"
                E.w(eng, "pe", last)
                if eng == "v":
                    ev = E.do("v", lambda v, bank=bank, p=p: v.tensor_copy(
                        p2_attT[:, p * 1024:(p + 1) * 1024], bank[:]), "dv")
                    E.mark(("ptb",), "dv", ev)
                    state["ev_of_pair"][p] = ("dv", ev)
                else:
                    ev = E.do("s", lambda s, bank=bank, p=p: s.copy(
                        p2_attT[:, p * 1024:(p + 1) * 1024], bank[:]), "ac")
                    E.mark(("ptb",), "ac", ev)
                    state["ev_of_pair"][p] = ("ac", ev)

            def emit_D(kt):
                sem, val = state["ev_of_pair"][kt // 2]
                E.w("t", sem, val)
                if kt == 0:
                    E.gate("t", ("p2y",))
                state["av_last"] = E.do("t", lambda t, kt=kt, b=hb: t.matmul(
                    PY[:], p2_v[b][:, kt * 128:(kt + 1) * 128],
                    p2_attT[:, kt * 512:(kt + 1) * 512],
                    start=(kt == 0), stop=(kt == nkt - 1)),
                    "pe" if kt == nkt - 1 else None)
                E.mark(("p2attT", kt // 2), "pe", E.cnt["pe"])

            def yevac():
                yb = g_ % 2
                E.gate("v", ("p2yst", yb))
                E.w("v", "pe", state["av_last"])
                ye = E.do("v", lambda v, yb=yb: v.tensor_copy(p2_yst[yb][:], PY[:]), "dv")
                E.mark(("p2y",), "dv", ye)
                E.w("q", "dv", ye)
                yw = E.do("q", lambda q, h_=h_, g_=g_, yb=yb: q.dma_start(
                    out=yt_s[h_, 4 * g_:4 * g_ + 4].rearrange("tt hd q -> hd tt q"),
                    in_=p2_yst[yb][:].rearrange("p (tt q) -> p tt q", q=128)), "sst", 16)
                E.mark(("p2yst", yb), "sst", yw)
                gstate[g_]["cd_pe"] = E.cnt["pe"]

            steps = [head]
            npair = nkt // 2
            for p in range(npair):
                steps.append(lambda p=p: emit_C(p))
                if p >= 1:
                    steps.append(lambda p=p: (emit_D(2 * (p - 1)), emit_D(2 * p - 1)))
            steps.append(lambda: (emit_D(nkt - 2), emit_D(nkt - 1)))
            steps.append(yevac)
            return steps

        for st in b_steps(0):
            st()
        for g_ in range(4):
            bs = b_steps(g_ + 1) if g_ < 3 else []
            cs = cd_steps(g_)
            nb, ncd = len(bs), len(cs)
            bi = 0
            for ci, c in enumerate(cs):
                c()
                want = min(nb, (ci + 1) * 2 * nb // ncd)
                while bi < want:
                    bs[bi]()
                    bi += 1
            while bi < nb:
                bs[bi]()
                bi += 1
        E.mark(("p2kt", hb), "pe", E.cnt["pe"])
        E.mark(("p2qt",), "pe", E.cnt["pe"])
        E.mark(("p2v",), "pe", E.cnt["pe"])

    E.barrier()

    for i in range(4):
        E.do("g", lambda g, i=i: g.dma_start(
            out=p3_wp[:].rearrange("p (dt c) -> p dt c", c=D)[:, i * 4:(i + 1) * 4, :],
            in_=wproj[i * 4:(i + 1) * 4].rearrange("dt p c -> p dt c")), "gld", 16)
    wp_ld = E.cnt["gld"]
    E.do("q", lambda q: q.dma_start(out=p3_bp[:], in_=bproj.broadcast_to([128, D])), "qld", 16)
    bp_ld = E.cnt["qld"]

    for tt in range(NTT):
        buf = tt % 2
        E.gate("q", ("p3yt", buf))
        yv = E.do("q", lambda q, tt=tt, buf=buf: q.dma_start(
            out=p3_yt[buf][:].rearrange("p (ct q) -> p ct q", q=128),
            in_=yt_s[:, tt].rearrange("ct hd q -> hd ct q")), "qld", 16)
        E.gate("q", ("p3x", buf))
        xv = E.do("q", lambda q, tt=tt, buf=buf: q.dma_start(
            out=p3_x[buf][:], in_=x_own[tt * 128:(tt + 1) * 128, :]), "qld", 16)
        E.gate("v", ("p3x1", buf))
        last_e2 = None
        for jc in range(4):
            bank_i = jc % 4
            bank = fbanks[bank_i]
            E.gate("t", ("fb", bank_i))
            E.w("t", "qld", yv)
            E.w("t", "gld", wp_ld)
            last = None
            for ct in range(NDT):
                last = E.do("t", lambda t, bank=bank, buf=buf, ct=ct, jc=jc:
                            t.matmul(bank, p3_yt[buf][:, ct * 128:(ct + 1) * 128],
                                     p3_wp[:, ct * D + jc * 512: ct * D + (jc + 1) * 512],
                                     start=(ct == 0), stop=(ct == NDT - 1)),
                            "pe" if ct == NDT - 1 else None)
            E.w("v", "pe", last)
            E.w("v", "qld", max(bp_ld, xv))
            e1 = E.do("v", lambda v, bank=bank, buf=buf, jc=jc: v.tensor_add(
                p3_x1[buf][:, jc * 512:(jc + 1) * 512], bank,
                p3_bp[:, jc * 512:(jc + 1) * 512]), "dv")
            E.w("v", "dv", e1)
            e2 = E.do("v", lambda v, buf=buf, jc=jc: v.tensor_add(
                p3_x1[buf][:, jc * 512:(jc + 1) * 512],
                p3_x1[buf][:, jc * 512:(jc + 1) * 512],
                p3_x[buf][:, jc * 512:(jc + 1) * 512]), "dv")
            E.mark(("fb", bank_i), "dv", e2)
            last_e2 = e2
        E.mark(("p3yt", buf), "pe", E.cnt["pe"])
        E.mark(("p3x", buf), "dv", last_e2)
        E.w("q", "dv", last_e2)
        E.do("q", lambda q, tt=tt, buf=buf: q.dma_start(
            out=x1_s[tt], in_=p3_x1[buf][:]), "sst", 16)
        s2v = E.do("q", lambda q, tt=tt, buf=buf: q.dma_start(
            out=out[tt * 128:(tt + 1) * 128, :], in_=p3_x1[buf][:]), "sst", 16)
        E.mark(("p3x1", buf), "sst", s2v)

    E.barrier()

    ht_store_val = {}
    cw_val = {}
    for tt in range(NTT):
        buf = tt % 2
        E.gate("q", ("p4x", buf))
        xv = E.do("q", lambda q, tt=tt, buf=buf: q.dma_start(
            out=p4_x[buf][:], in_=x1_s[tt]), "qld", 16)
        E.gate("v", ("p4h", buf))
        vn = emit_ln_tile(p4_x[buf][:], "qld", xv, p4_h[buf][:], p4_junk[:])
        E.mark(("p4x", buf), "dv", vn)
        E.w("t", "dv", vn)
        ht_done = None
        for half in range(2):
            bank = tbanks[half]
            E.gate("t", ("ptb",))
            last = None
            for k in range(8):
                dt = half * 8 + k
                last = E.do("t", lambda t, buf=buf, dt=dt, bank=bank, k=k: t.transpose(
                    bank[:, k * 128:(k + 1) * 128],
                    p4_h[buf][:, dt * 128:(dt + 1) * 128], ident[:]),
                    "pe" if k == 7 else None)
            E.w("v", "pe", last)
            E.gate("v", ("p4ht", buf))
            E.gate("v", ("p4htS", buf))
            ev = E.do("v", lambda v, half=half, bank=bank, buf=buf: v.tensor_copy(
                p4_ht[buf][:, half * 1024:(half + 1) * 1024], bank[:]), "dv")
            E.mark(("ptb",), "dv", ev)
            if half == 1:
                E.mark(("p4h", buf), "dv", ev)
                ht_done = ev
        E.w("q", "dv", ht_done)
        hts = E.do("q", lambda q, tt=tt, buf=buf: q.dma_start(
            out=ht_s[:, :, tt * 128:(tt + 1) * 128].rearrange("dt d c -> d dt c"),
            in_=p4_ht[buf][:].rearrange("p (dt c) -> p dt c", c=128)), "sst", 16)
        E.mark(("p4htS", buf), "sst", hts)
        ht_store_val[tt] = hts
        E.gate("t", ("p4pc",))
        E.w("t", "dv", ht_done)
        E.w("t", "gld", const_gld)
        last = None
        for dt in range(NDT):
            last = E.do("t", lambda t, buf=buf, dt=dt: t.matmul(
                PY[:, 0:NE], p4_ht[buf][:, dt * 128:(dt + 1) * 128],
                gwt_sb[:, dt * NE:(dt + 1) * NE],
                start=(dt == 0), stop=(dt == NDT - 1)),
                "pe" if dt == NDT - 1 else None)
        E.mark(("p4ht", buf), "pe", last)
        E.w("v", "pe", last)
        r1 = E.do("v", lambda v: v.reduce_max(rt_m[:], PY[:, 0:NE], axis=AX.X, negate=True), "dv")
        E.w("s", "dv", r1)
        r2 = E.do("s", lambda s: s.activation(rt_sm[:], PY[:, 0:NE], AF.Exp,
                                              bias=rt_m[:], accum_out=rt_s[:]), "ac")
        E.mark(("p4pc",), "ac", r2)
        E.w("v", "ac", r2)
        r3 = E.do("v", lambda v: v.reciprocal(rt_m[:], rt_s[:]), "dv")
        E.w("v", "dv", r3)
        r4 = E.do("v", lambda v, tt=tt: v.tensor_scalar(
            cw_sb[:, tt * NE:(tt + 1) * NE], rt_sm[:], rt_m[:], None, ALU.mult), "dv")
        E.w("v", "dv", r4)
        r5 = E.do("v", lambda v, tt=tt: v.tensor_copy(
            rt_wk[:], cw_sb[:, tt * NE:(tt + 1) * NE]), "dv")
        E.w("v", "dv", r5)
        for it in range(TOPK - 1):
            a1 = E.do("v", lambda v: v.reduce_max(rt_m[:], rt_wk[:], axis=AX.X), "dv")
            E.w("v", "dv", a1)
            a2 = E.do("v", lambda v: v.tensor_scalar(
                rt_ge[:], rt_wk[:], rt_m[:], None, ALU.is_ge), "dv")
            E.w("v", "dv", a2)
            a3 = E.do("v", lambda v: v.scalar_tensor_tensor(
                rt_wk[:], rt_ge[:], -2.0, rt_wk[:], ALU.mult, ALU.add), "dv")
            E.w("v", "dv", a3)
        b1 = E.do("v", lambda v: v.reduce_max(rt_m[:], rt_wk[:], axis=AX.X), "dv")
        E.w("v", "dv", b1)
        b2 = E.do("v", lambda v, tt=tt: v.tensor_scalar(
            rt_ge[:], cw_sb[:, tt * NE:(tt + 1) * NE], rt_m[:], None, ALU.is_ge), "dv")
        E.w("v", "dv", b2)
        cwv = E.do("v", lambda v, tt=tt: v.tensor_mul(
            cw_sb[:, tt * NE:(tt + 1) * NE], cw_sb[:, tt * NE:(tt + 1) * NE],
            rt_ge[:]), "dv")
        cw_val[tt] = cwv

    PANELS = [(0, 256), (256, 256), (512, 256), (768, 256), (1024, 256), (1280, 128)]
    POFF = [0, 4096, 8192, 12288, 16384, 20480]
    p3_out_stores = E.cnt["sst"]
    first_accum = True

    panel_uv = {}

    def emit_panel_pair(gi_, pi):
        ee = gi_ % NE
        m0, mw = PANELS[pi]
        off = POFF[pi]
        pb = pi % 2
        E.gate("g", ("p5wg", pb))
        E.do("g", lambda g, ee=ee, off=off, mw=mw, pb=pb: g.dma_start(
            out=p5_wg[pb][:, 0:NDT * mw],
            in_=wg_t[ee, :, off:off + NDT * mw]), "gld", 16)
        E.gate("g", ("p5wu", pb))
        panel_uv[(gi_, pi)] = E.do("g", lambda g, ee=ee, off=off, mw=mw, pb=pb: g.dma_start(
            out=p5_wu[pb][:, 0:NDT * mw],
            in_=wu_t[ee, :, off:off + NDT * mw]), "gld", 16)

    for ch in range(2):
        E.gate("q", ("p5ht",))
        E.w("q", "sst", ht_store_val[ch * 8 + 7])
        hv5 = E.do("q", lambda q, ch=ch: q.dma_start(
            out=p5_ht[:].rearrange("p (dt c) -> p dt c", c=1024),
            in_=ht_s[:, :, ch * 1024:(ch + 1) * 1024].rearrange("dt d c -> d dt c")),
            "qld", 16)
        for e in range(NE):
            gidx = ch * NE + e
            E.gate("g", ("p5wd", 0))
            wd_v = {}
            wd_v[0] = E.do("g", lambda g, e=e: g.dma_start(
                out=p5_wd[0][:], in_=wd_t[e, :, 0:11264]), "gld", 16)
            zlast = {}
            for pi, (m0, mw) in enumerate(PANELS):
                pb = pi % 2
                if (gidx, pi) not in panel_uv:
                    emit_panel_pair(gidx, pi)
                uv = panel_uv[(gidx, pi)]
                if pi == 3:
                    E.gate("g", ("p5wd", 1))
                    wd_v[1] = E.do("g", lambda g, e=e: g.dma_start(
                        out=p5_wd[1][:], in_=wd_t[e, :, 11264:22528]), "gld", 16)
                for mm_j in range(mw // 128):
                    mt = (m0 + mm_j * 128) // 128
                    E.gate("t", ("p5pa",))
                    E.w("t", "gld", uv)
                    E.w("t", "qld", hv5)
                    lg = None
                    for dt in range(NDT):
                        wgs = p5_wg[pb][:, dt * mw + mm_j * 128: dt * mw + mm_j * 128 + 128]
                        for th in range(2):
                            lg = E.do("t", lambda t, wgs=wgs, dt=dt, th=th: t.matmul(
                                PA[:, th * 512:(th + 1) * 512], wgs,
                                p5_ht[:, dt * 1024 + th * 512: dt * 1024 + (th + 1) * 512],
                                start=(dt == 0), stop=(dt == NDT - 1)),
                                "pe" if (dt == NDT - 1 and th == 1) else None)
                    sgb = mt % 2
                    E.gate("s", ("p5sg", sgb))
                    E.w("s", "pe", lg)
                    sv = E.do("s", lambda s, sgb=sgb: s.activation(
                        p5_sg[sgb][:], PA[:], AF.Silu), "ac")
                    E.mark(("p5pa",), "ac", sv)
                    E.gate("t", ("p5pb",))
                    lu = None
                    for dt in range(NDT):
                        wus = p5_wu[pb][:, dt * mw + mm_j * 128: dt * mw + mm_j * 128 + 128]
                        for th in range(2):
                            lu = E.do("t", lambda t, wus=wus, dt=dt, th=th: t.matmul(
                                PB[:, th * 512:(th + 1) * 512], wus,
                                p5_ht[:, dt * 1024 + th * 512: dt * 1024 + (th + 1) * 512],
                                start=(dt == 0), stop=(dt == NDT - 1)),
                                "pe" if (dt == NDT - 1 and th == 1) else None)
                    E.gate("v", ("p5z", mt))
                    E.w("v", "ac", sv)
                    E.w("v", "pe", lu)
                    zv = E.do("v", lambda v, sgb=sgb, mt=mt: v.tensor_mul(
                        p5_z[:, mt * 1024:(mt + 1) * 1024], p5_sg[sgb][:], PB[:]), "dv")
                    E.mark(("p5pb",), "dv", zv)
                    E.mark(("p5sg", sgb), "dv", zv)
                    zlast[mt] = zv
                E.mark(("p5wg", pb), "pe", E.cnt["pe"])
                E.mark(("p5wu", pb), "pe", E.cnt["pe"])

            if gidx + 1 < 2 * NE:
                emit_panel_pair(gidx + 1, 0)
                emit_panel_pair(gidx + 1, 1)

            bank_sets = [(PA[:, 0:512], PA[:, 512:1024], ("p5pa",)),
                         (PB[:, 0:512], PB[:, 512:1024], ("p5pb",)),
                         (PS3[:, 0:512], PS3[:, 512:1024], ("p5pcd",))]
            for jh in range(2):
                wb = jh
                dv5 = wd_v[jh]
                for ts in range(8):
                    tt = ch * 8 + ts
                    bs = (jh * 8 + ts) % 3
                    bk0, bk1, bres = bank_sets[bs]
                    E.gate("t", bres)
                    E.w("t", "gld", dv5)
                    l1 = None
                    for mt in range(MT):
                        E.w("t", "dv", zlast[mt])
                        E.do("t", lambda t, mt=mt, ts=ts, bk0=bk0, wb=wb: t.matmul(
                            bk0, p5_z[:, mt * 1024 + ts * 128: mt * 1024 + ts * 128 + 128],
                            p5_wd[wb][:, mt * 1024: mt * 1024 + 512],
                            start=(mt == 0), stop=(mt == MT - 1)),
                            None)
                        l1 = E.do("t", lambda t, mt=mt, ts=ts, bk1=bk1, wb=wb: t.matmul(
                            bk1, p5_z[:, mt * 1024 + ts * 128: mt * 1024 + ts * 128 + 128],
                            p5_wd[wb][:, mt * 1024 + 512: mt * 1024 + 1024],
                            start=(mt == 0), stop=(mt == MT - 1)),
                            "pe" if mt == MT - 1 else None)
                    evb = (jh * 8 + ts) % 3
                    E.gate("v", ("p5ev", evb))
                    E.w("v", "pe", l1)
                    E.w("v", "dv", cw_val[tt])
                    cwcol = cw_sb[:, tt * NE + e: tt * NE + e + 1]
                    E.do("v", lambda v, evb=evb, bk0=bk0, cwcol=cwcol: v.tensor_scalar(
                        p5_ev[evb][:, 0:512], bk0, cwcol, None, ALU.mult), "dv")
                    x2v = E.do("v", lambda v, evb=evb, bk1=bk1, cwcol=cwcol: v.tensor_scalar(
                        p5_ev[evb][:, 512:1024], bk1, cwcol, None, ALU.mult), "dv")
                    E.mark(bres, "dv", x2v)
                    E.w("g", "dv", x2v)
                    if first_accum:
                        E.w("g", "sst", p3_out_stores)
                        first_accum = False
                    av = E.do("g", lambda g, tt=tt, jh=jh, evb=evb: g.dma_start(
                        out=out[tt * 128:(tt + 1) * 128, jh * 1024:(jh + 1) * 1024],
                        in_=p5_ev[evb][:], accum_op=ALU.add), "gst", 16)
                    E.mark(("p5ev", evb), "gst", av)
                E.mark(("p5wd", wb), "pe", E.cnt["pe"])
            for mt in range(MT):
                E.mark(("p5z", mt), "pe", E.cnt["pe"])
        E.mark(("p5ht",), "pe", E.cnt["pe"])

    E.barrier()

    with nc.Block() as blk:
        blk.tensor(lambda t: [f(t) for f in E.ops["t"]])
        blk.vector(lambda v: [f(v) for f in E.ops["v"]])
        blk.scalar(lambda s: [f(s) for f in E.ops["s"]])
        blk.sync(lambda q: [f(q) for f in E.ops["q"]])
        blk.gpsimd(lambda g: [f(g) for f in E.ops["g"]])
    return nc


def make_masks(h_par):
    m = np.zeros((128, NTT * 1024), dtype=np.float32)
    r = np.arange(128)[:, None]
    c = np.arange(1024)[None, :]
    for i in range(NTT):
        jq = 2 * i + h_par
        qg = jq * 128 + r
        ub = i // 4
        kpos = ub * 1024 + c
        pb = kpos // 128
        ob = (pb ^ 1) if h_par == 1 else pb
        korig = ob * 128 + kpos % 128
        vis = korig <= qg
        m[:, i * 1024:(i + 1) * 1024] = np.where(vis, 0.0, -1e30).astype(np.float32)
    return m.astype(BF)


def kernel(L0_attn_x, L0_attn_w_qkv, L0_attn_b_qkv, L0_attn_w_proj, L0_attn_b_proj,
           L0_ffn_gate_w, L0_ffn_w_gate, L0_ffn_w_up, L0_ffn_w_down):
    x = np.ascontiguousarray(np.asarray(L0_attn_x, dtype=np.float32))
    wqkv = np.ascontiguousarray(
        np.asarray(L0_attn_w_qkv, dtype=np.float32).reshape(NDT, 128, 24, 256)
        .transpose(1, 2, 0, 3).reshape(128, 24 * NDT * 256))
    bqkv = np.ascontiguousarray(np.asarray(L0_attn_b_qkv, dtype=np.float32).reshape(48, 128).T)
    wproj = np.ascontiguousarray(np.asarray(L0_attn_w_proj, dtype=np.float32)).reshape(NDT, 128, D)
    bproj = np.ascontiguousarray(np.asarray(L0_attn_b_proj, dtype=np.float32).reshape(1, D))
    gate_w = np.asarray(L0_ffn_gate_w, dtype=np.float32)
    gwt = np.ascontiguousarray(
        gate_w.T.reshape(NDT, 128, NE).transpose(1, 0, 2).reshape(128, NDT * NE))
    PANELS = [(0, 256), (256, 256), (512, 256), (768, 256), (1024, 256), (1280, 128)]
    def pack_gu(w):
        w3 = np.asarray(w, dtype=np.float32).transpose(0, 2, 1).reshape(NE, NDT, 128, MD)
        return np.ascontiguousarray(np.concatenate(
            [w3[:, :, :, m0:m0 + pw].transpose(0, 2, 1, 3).reshape(NE, 128, NDT * pw)
             for m0, pw in PANELS], axis=2))
    wg_t = pack_gu(L0_ffn_w_gate)
    wu_t = pack_gu(L0_ffn_w_up)
    wd3 = np.asarray(L0_ffn_w_down, dtype=np.float32).transpose(0, 2, 1).reshape(NE, MT, 128, D)
    wd_t = np.ascontiguousarray(np.concatenate(
        [wd3[:, :, :, jh * 1024:(jh + 1) * 1024].transpose(0, 2, 1, 3).reshape(NE, 128, MT * 1024)
         for jh in range(2)], axis=2))
    ident = np.eye(128, dtype=BF)

    nc = build_program()

    swap = np.arange(T // 128).reshape(-1, 2)[:, ::-1].reshape(-1)
    in_maps = []
    for core in range(8):
        b, h_par = core // 2, core % 2
        xb = x[b]
        if h_par == 1:
            x_perm = np.ascontiguousarray(xb.reshape(T // 128, 128, D)[swap].reshape(T, D))
        else:
            x_perm = xb
        x_o = np.ascontiguousarray(
            xb.reshape(T // 128, 128, D)[h_par::2].reshape(OWN, D))
        in_maps.append({
            "x_b": x_perm, "x_own": x_o,
            "wqkv": wqkv, "bqkv": bqkv, "wproj": wproj, "bproj": bproj,
            "gwt": gwt, "wg_t": wg_t, "wu_t": wu_t, "wd_t": wd_t,
            "masks": make_masks(h_par), "ident": ident,
        })
    res = run_bass_kernel_spmd(nc, in_maps, core_ids=list(range(8)))

    full = np.empty((B, T, D), dtype=np.float32)
    for core in range(8):
        b, h_par = core // 2, core % 2
        o = np.asarray(res.results[core]["out"])
        for tt in range(NTT):
            full[b, (2 * tt + h_par) * 128:(2 * tt + h_par + 1) * 128, :] = \
                o[tt * 128:(tt + 1) * 128, :]
    return full


# version 16
# speedup vs baseline: 1.0670x; 1.0259x over previous
import math
import numpy as np
import ml_dtypes
import concourse.bass as bass
import concourse.mybir as mybir
from concourse.bass_utils import run_bass_kernel_spmd

F32 = mybir.dt.float32
BF16 = mybir.dt.bfloat16
AF = mybir.ActivationFunctionType
ALU = mybir.AluOpType
AX = mybir.AxisListType

B, T, D = 4, 4096, 2048
H, HD = 16, 128
NE, MD = 16, 1408
MT = MD // 128
TOPK = 6
ATTN_MULT = 1.0 / math.sqrt(HD)
NEG_BIG = -1e30 / ATTN_MULT
EPS = 1e-5
OWN = 2048
NTT = OWN // 128
NDT = D // 128

BF = ml_dtypes.bfloat16


class Emitter:
    ENGINES = ("t", "v", "s", "q", "g")

    def __init__(self, nc, sem_names):
        self.nc = nc
        self.sems = {n: nc.alloc_semaphore(n) for n in sem_names}
        self.cnt = {n: 0 for n in sem_names}
        self.ops = {e: [] for e in self.ENGINES}
        self.free = {}

    def _sem(self, n):
        if n not in self.sems:
            self.sems[n] = self.nc.alloc_semaphore(n)
            self.cnt[n] = 0
        return self.sems[n]

    def w(self, e, sem, val=None):
        self._sem(sem)
        v = self.cnt[sem] if val is None else val
        if v is None or v <= 0:
            return
        h = self.sems[sem]
        self.ops[e].append(lambda eng, h=h, v=v: eng.wait_ge(h, v))

    def do(self, e, fn, inc=None, n=1):
        if inc is None:
            self.ops[e].append(fn)
            return None
        self._sem(inc)
        self.cnt[inc] += n
        h = self.sems[inc]
        v = self.cnt[inc]
        self.ops[e].append(lambda eng, fn=fn, h=h, n=n: fn(eng).then_inc(h, n))
        return v

    def mark(self, res, sem, val):
        self.free[res] = (sem, val)

    def gate(self, e, res):
        if res in self.free:
            sem, val = self.free[res]
            self.w(e, sem, val)

    def begin_defer(self, e):
        self._saved = self.ops[e]
        self.ops[e] = []

    def end_defer(self, e):
        cap = self.ops[e]
        self.ops[e] = self._saved
        return cap

    def barrier(self):
        snapshot = [(n, self.cnt[n]) for n in self.sems if self.cnt[n] > 0]
        for e in self.ENGINES:
            for n, v in snapshot:
                self.w(e, n, v)


def build_program():
    nc = bass.Bass("TRN2", target_bir_lowering=False)

    x_in = nc.dram_tensor("x_b", [T, D], F32, kind="ExternalInput").ap()
    x_own = nc.dram_tensor("x_own", [OWN, D], F32, kind="ExternalInput").ap()
    wqkv = nc.dram_tensor("wqkv", [128, 24 * NDT * 256], F32, kind="ExternalInput").ap()
    bqkv = nc.dram_tensor("bqkv", [128, 48], F32, kind="ExternalInput").ap()
    wproj = nc.dram_tensor("wproj", [NDT, 128, D], F32, kind="ExternalInput").ap()
    bproj = nc.dram_tensor("bproj", [1, D], F32, kind="ExternalInput").ap()
    gwt = nc.dram_tensor("gwt", [128, NDT * NE], F32, kind="ExternalInput").ap()
    wg_t = nc.dram_tensor("wg_t", [NE, 128, NDT * MD], F32, kind="ExternalInput").ap()
    wu_t = nc.dram_tensor("wu_t", [NE, 128, NDT * MD], F32, kind="ExternalInput").ap()
    wd_t = nc.dram_tensor("wd_t", [NE, 128, 2 * MT * 1024], F32, kind="ExternalInput").ap()
    masks = nc.dram_tensor("masks", [128, NTT * 1024], BF16, kind="ExternalInput").ap()
    ident_in = nc.dram_tensor("ident", [128, 128], BF16, kind="ExternalInput").ap()
    out = nc.dram_tensor("out", [OWN, D], F32, kind="ExternalOutput").ap()

    qt_s = nc.dram_tensor("qt_s", [H, 128, OWN], BF16, kind="Internal").ap()
    kt_s = nc.dram_tensor("kt_s", [H, 128, T], BF16, kind="Internal").ap()
    vt_s = nc.dram_tensor("vt_s", [H, 128, T], BF16, kind="Internal").ap()
    x1_s = nc.dram_tensor("x1_s", [NTT, 128, D], F32, kind="Internal").ap()
    ht_s = nc.dram_tensor("ht_s", [NDT, 128, OWN], BF16, kind="Internal").ap()
    yt_s = nc.dram_tensor("yt_s", [H, NTT, 128, 128], BF16, kind="Internal").ap()

    SBASE = 16512
    def sbt(name, shape, dtype, off):
        return nc.alloc_sbuf_tensor_at(name, shape, dtype, offset=SBASE + off).ap()

    ident = sbt("identb", [128, 128], BF16, 0)
    bq_sb = sbt("bq_sb", [128, 48], F32, 256)
    gwt_sb = sbt("gwt_sb", [128, NDT * NE], BF16, 448)
    cw_sb = sbt("cw_sb", [128, NTT * NE], F32, 960)
    recip_sb = sbt("recip_sb", [128, NTT], F32, 1984)
    st_sums = sbt("st_sums", [128, 1], F32, 2048)
    st_ssq = sbt("st_ssq", [128, 1], F32, 2080)
    st_nm = sbt("st_nm", [128, 1], F32, 2112)
    st_var = sbt("st_var", [128, 1], F32, 2144)
    st_msq = sbt("st_msq", [128, 1], F32, 2176)
    st_lnv = sbt("st_lnv", [128, 1], F32, 2208)
    st_rstd = sbt("st_rstd", [128, 1], F32, 2240)
    st_stot = sbt("st_stot", [128, 1], F32, 2272)
    mtmp = sbt("mtmp", [128, 8], F32, 2304)
    rall = sbt("rall", [128, 8], F32, 2336)
    negr = sbt("negr", [128, 8], F32, 2368)
    chsum = sbt("chsum", [128, 8], F32, 2400)
    fixall = sbt("fixall", [128, 8], F32, 2432)
    fsum = sbt("fsum", [128, 8], F32, 2464)
    rt_sm = sbt("rt_sm", [128, NE], F32, 2496)
    rt_wk = sbt("rt_wk", [128, NE], F32, 2560)
    rt_ge = sbt("rt_ge", [128, NE], F32, 2624)
    rt_m = sbt("rt_m", [128, 1], F32, 2688)
    rt_s = sbt("rt_s", [128, 1], F32, 2720)
    A0 = 4096

    ln1T = sbt("ln1T", [128, NDT * T], BF16, A0)
    p1_x = [sbt(f"p1x{i}", [128, D], F32, A0 + 131072 + i * 8192) for i in range(2)]
    p1_ln = [sbt(f"p1ln{i}", [128, D], BF16, A0 + 147456 + i * 4096) for i in range(2)]
    p1_junk = sbt("p1junk", [128, D], BF16, A0 + 155648)
    p1_w = [sbt(f"p1w{i}", [128, NDT * 256], BF16, A0 + 159744 + i * 8192) for i in range(2)]
    p1_st = [sbt(f"p1st{i}", [128, 512], BF16, A0 + 176128 + i * 1024) for i in range(4)]

    mask_sb = sbt("mask_sb", [128, NTT * 1024], BF16, A0)
    p2_kt = [sbt(f"p2kt{i}", [128, T], BF16, A0 + 65536 + i * 8192) for i in range(2)]
    p2_vt = [sbt(f"p2vt{i}", [128, T], BF16, A0 + 81920 + i * 8192) for i in range(2)]
    _p2qt = sbt("p2qt0", [128, OWN], BF16, A0 + 98304)
    p2_qt = [_p2qt, _p2qt]
    _p2v = sbt("p2v0", [128, T], BF16, A0 + 106496)
    p2_v = [_p2v, _p2v]
    p2_att = sbt("p2att", [128, 28672], BF16, A0 + 114688)
    p2_attT = sbt("p2attT", [128, 32 * 512], BF16, A0 + 172032)
    p2_yst = [sbt(f"p2yst{i}", [128, 512], BF16, A0 + 204800 + i * 1024) for i in range(2)]

    p3_wp = sbt("p3wp", [128, NDT * D], BF16, A0)
    p3_bp = sbt("p3bp", [128, D], F32, A0 + 65536)
    p3_yt = [sbt(f"p3yt{i}", [128, H * 128], BF16, A0 + 73728 + i * 4096) for i in range(2)]
    p3_x = [sbt(f"p3x{i}", [128, D], F32, A0 + 81920 + i * 8192) for i in range(2)]
    p3_x1 = [sbt(f"p3x1{i}", [128, D], F32, A0 + 98304 + i * 8192) for i in range(2)]

    p4_x = [sbt(f"p4x{i}", [128, D], F32, A0 + i * 8192) for i in range(2)]
    p4_h = [sbt(f"p4h{i}", [128, D], BF16, A0 + 16384 + i * 4096) for i in range(2)]
    p4_ht = [sbt(f"p4ht{i}", [128, NDT * 128], BF16, A0 + 24576 + i * 4096) for i in range(2)]
    p4_junk = sbt("p4junk", [128, D], BF16, A0 + 32768)

    p5_ht = sbt("p5ht", [128, NDT * 1024], BF16, A0)
    p5_z = sbt("p5z", [128, MT * 1024], BF16, A0 + 32768)
    p5_wg = [sbt(f"p5wg{i}", [128, NDT * 256], BF16, A0 + 55296 + i * 8192) for i in range(2)]
    p5_wu = [sbt(f"p5wu{i}", [128, NDT * 256], BF16, A0 + 71680 + i * 8192) for i in range(2)]
    p5_wd = [sbt(f"p5wd{i}", [128, MT * 1024], BF16, A0 + 88064 + i * 22528) for i in range(2)]
    p5_sg = [sbt(f"p5sg{i}", [128, 1024], BF16, A0 + 133120 + i * 2048) for i in range(2)]
    p5_ev = [sbt(f"p5ev{i}", [128, 1024], F32, A0 + 137216 + i * 4096) for i in range(3)]

    PA = nc.alloc_psum_tensor("PA", [128, 1024], F32).ap()
    PB = nc.alloc_psum_tensor("PB", [128, 1024], F32).ap()
    PS3 = nc.alloc_psum_tensor("PS3", [128, 1024], F32).ap()
    PY = nc.alloc_psum_tensor("PY", [128, 512], F32).ap()
    PTA = nc.alloc_psum_tensor("PTA", [128, 1024], BF16).ap()
    fbanks = [PA[:, 0:512], PA[:, 512:1024], PB[:, 0:512], PB[:, 512:1024]]
    tbanks = [PTA, PTA]
    sunits = [PA, PB, PS3]

    E = Emitter(nc, ["gld", "qld", "sst", "gst", "pe", "dv", "ac", "ast"])

    E.do("q", lambda q: q.dma_start(out=ident[:], in_=ident_in[:]), "c_id", 16)
    E.do("q", lambda q: q.dma_start(out=bq_sb[:], in_=bqkv[:]), "c_bq", 16)
    E.do("g", lambda g: g.dma_start(out=gwt_sb[:], in_=gwt[:]), "c_gw", 16)
    const_gld = E.cnt["c_gw"]

    def emit_ln_tile(xap, xload_sem, xload_val, lnout, junk):
        E.w("v", xload_sem, xload_val)
        E.do("v", lambda v, xap=xap: v.reduce_sum(st_sums[:], xap, axis=AX.X), "dv")
        v1 = E.do("v", lambda v, xap=xap, junk=junk: v.scalar_tensor_tensor(
            junk, xap, 1.0, xap, ALU.mult, ALU.mult, accum_out=st_ssq[:]), "dv")
        E.w("v", "dv", v1)
        E.do("v", lambda v: v.tensor_scalar(st_nm[:], st_sums[:], -1.0 / D, None, ALU.mult), "dv")
        v2 = E.do("v", lambda v: v.tensor_scalar(
            st_var[:], st_ssq[:], 1.0 / D, EPS, ALU.mult, ALU.add), "dv")
        E.w("v", "dv", v2)
        v3 = E.do("v", lambda v: v.scalar_tensor_tensor(
            st_msq[:], st_nm[:], 1.0, st_nm[:], ALU.mult, ALU.mult), "dv")
        E.w("v", "dv", v3)
        v4 = E.do("v", lambda v: v.tensor_sub(st_var[:], st_var[:], st_msq[:]), "dv")
        E.w("s", "dv", v4)
        s1 = E.do("s", lambda s: s.activation(st_lnv[:], st_var[:], AF.Ln), "ac")
        E.w("s", "ac", s1)
        s2 = E.do("s", lambda s: s.activation(st_rstd[:], st_lnv[:], AF.Exp, scale=-0.5), "ac")
        E.w("v", "ac", s2)
        vn = E.do("v", lambda v, xap=xap, lnout=lnout: v.tensor_scalar(
            lnout, xap, st_nm[:], st_rstd[:], ALU.add, ALU.mult), "dv")
        return vn

    lnv = {}
    for tt in range(T // 128):
        buf = tt % 2
        E.gate("q", ("p1x", buf))
        xv = E.do("q", lambda q, tt=tt, buf=buf: q.dma_start(
            out=p1_x[buf][:], in_=x_in[tt * 128:(tt + 1) * 128, :]), f"p1x{buf}", 16)
        E.gate("v", ("p1ln", buf))
        vn = emit_ln_tile(p1_x[buf][:], f"p1x{buf}", xv, p1_ln[buf][:], p1_junk[:])
        E.mark(("p1x", buf), "dv", vn)
        E.w("t", "dv", vn)
        E.w("t", "c_id", 16)
        for half in range(2):
            bank = tbanks[half]
            E.gate("t", ("ptb",))
            last = None
            for k in range(8):
                dt = half * 8 + k
                last = E.do("t", lambda t, buf=buf, dt=dt, bank=bank, k=k: t.transpose(
                    bank[:, k * 128:(k + 1) * 128],
                    p1_ln[buf][:, dt * 128:(dt + 1) * 128], ident[:]),
                    "pe" if k == 7 else None)
            E.w("v", "pe", last)
            ev = E.do("v", lambda v, half=half, bank=bank, tt=tt: v.tensor_copy(
                ln1T.rearrange("p (dt c) -> p dt c", c=T)[:, half * 8:(half + 1) * 8,
                                                         tt * 128:(tt + 1) * 128],
                bank.rearrange("p (dt c) -> p dt c", c=128)), "dv")
            E.mark(("ptb",), "dv", ev)
            if half == 1:
                E.mark(("p1ln", buf), "dv", ev)
                lnv[tt] = ev

    E.w("v", "c_bq", 16)
    E.w("s", "c_bq", 16)
    ln3 = ln1T.rearrange("p (dt i c) -> p dt i c", i=T // 128, c=128)
    evac_eng = 0
    for grp in range(24):
        wbuf = grp % 2
        E.gate("g", ("p1w", wbuf))
        wv = E.do("g", lambda g, grp=grp, wbuf=wbuf: g.dma_start(
            out=p1_w[wbuf][:],
            in_=wqkv[:, grp * 4096:(grp + 1) * 4096]), f"p1w{wbuf}", 16)
        last_mm_of_group = None
        for jj in range(2):
            jt = grp * 2 + jj
            kind = jt // 16
            ntc = 4 if kind == 0 else 8
            for tc in range(ntc):
                bank_i = (jt * 8 + tc) % 4
                bank = fbanks[bank_i]
                E.gate("t", ("fb", bank_i))
                E.w("t", f"p1w{wbuf}", wv)
                need_tt = (8 * tc + 7) if kind == 0 else (4 * tc + 3)
                E.w("t", "dv", lnv[need_tt])
                last = None
                for dt in range(NDT):
                    if kind == 0:
                        rhs = ln3[:, dt, 8 * tc: 8 * tc + 8: 2, :]
                    else:
                        rhs = ln1T[:, dt * T + tc * 512: dt * T + (tc + 1) * 512]
                    last = E.do("t", lambda t, bank=bank, wbuf=wbuf, dt=dt, jj=jj,
                                rhs=rhs, first=(dt == 0), fin=(dt == NDT - 1):
                                t.matmul(bank, p1_w[wbuf][:, dt * 256 + jj * 128:
                                                          dt * 256 + jj * 128 + 128],
                                         rhs, start=first, stop=fin),
                                "pe" if dt == NDT - 1 else None)
                last_mm_of_group = last
                st_i = (jt * 8 + tc) % 4
                if kind == 0:
                    eng = "v"
                else:
                    eng = "s" if evac_eng % 2 == 0 else "v"
                    evac_eng += 1
                E.gate(eng, ("p1st", st_i))
                E.w(eng, "pe", last)
                if kind == 0:
                    evv = E.do("v", lambda v, st_i=st_i, bank=bank, jt=jt: v.tensor_scalar(
                        p1_st[st_i][:], bank, bq_sb[:, jt:jt + 1], ATTN_MULT,
                        ALU.add, ALU.mult), "dv")
                    E.mark(("fb", bank_i), "dv", evv)
                    E.w("q", "dv", evv)
                elif eng == "s":
                    evv = E.do("s", lambda s, st_i=st_i, bank=bank, jt=jt: s.add(
                        p1_st[st_i][:], bank, bq_sb[:, jt:jt + 1]), "ac")
                    E.mark(("fb", bank_i), "ac", evv)
                    E.w("q", "ac", evv)
                else:
                    evv = E.do("v", lambda v, st_i=st_i, bank=bank, jt=jt: v.tensor_scalar(
                        p1_st[st_i][:], bank, bq_sb[:, jt:jt + 1], None, ALU.add), "dv")
                    E.mark(("fb", bank_i), "dv", evv)
                    E.w("q", "dv", evv)
                if kind == 0:
                    dst = qt_s[jt, :, tc * 512:(tc + 1) * 512]
                elif kind == 1:
                    dst = kt_s[jt - 16, :, tc * 512:(tc + 1) * 512]
                else:
                    dst = vt_s[jt - 32, :, tc * 512:(tc + 1) * 512]
                sv = E.do("q", lambda q, dst=dst, st_i=st_i: q.dma_start(
                    out=dst, in_=p1_st[st_i][:]), f"st{st_i}", 16)
                E.mark(("p1st", st_i), f"st{st_i}", sv)
        if last_mm_of_group is not None:
            E.mark(("p1w", wbuf), "pe", last_mm_of_group)

    E.barrier()

    E.do("q", lambda q: q.dma_start(out=mask_sb[:], in_=masks[:]), "msk", 16)
    mask_ld = E.cnt["msk"]

    def fetch_head(h_, buf):
        E.do("g", lambda g, h_=h_, buf=buf: g.dma_start(
            out=p2_kt[buf][:], in_=kt_s[h_]), f"p2t{buf}", 16)
        E.do("g", lambda g, h_=h_, buf=buf: g.dma_start(
            out=p2_vt[buf][:], in_=vt_s[h_]), f"p2t{buf}", 16)
        E.gate("g", ("p2qt",))
        return E.do("g", lambda g, h_=h_, buf=buf: g.dma_start(
            out=p2_qt[0][:], in_=qt_s[h_]), f"p2t{buf}", 16)

    def attbase(g_, s_):
        return (s_ * 3072) if g_ % 2 == 0 else (12288 + s_ * 4096)

    for h_ in range(H):
        hb = h_ % 2
        E.gate("g", ("p2kt", hb))
        hv = fetch_head(h_, hb)

        E.w("t", f"p2t{hb}", hv)
        for q8 in range(4):
            bank = tbanks[q8 % 2]
            E.gate("t", ("ptb",))
            last = None
            for k in range(8):
                kt = q8 * 8 + k
                last = E.do("t", lambda t, bank=bank, k=k, kt=kt, b=hb: t.transpose(
                    bank[:, k * 128:(k + 1) * 128],
                    p2_vt[b][:, kt * 128:(kt + 1) * 128], ident[:]),
                    "pe" if k == 7 else None)
            E.w("v", "pe", last)
            E.gate("v", ("p2v",))
            ev = E.do("v", lambda v, bank=bank, q8=q8, b=hb: v.tensor_copy(
                p2_v[b][:, q8 * 1024:(q8 + 1) * 1024], bank[:]), "dv")
            E.mark(("ptb",), "dv", ev)
        vtr_done = E.cnt["dv"]

        ucnt = [0]
        gstate = {}

        def b_steps(g_, hb=hb, ucnt=ucnt):
            nu = g_ + 1
            steps = []

            def grp_head():
                if g_ > 0:
                    E.w("v", "ac", gstate[g_ - 1]["exp_done"])
                if g_ >= 2:
                    E.w("s", "pe", gstate[g_ - 2]["cd_pe"])
                    E.w("v", "pe", gstate[g_ - 2]["cd_pe"])
            steps.append(grp_head)

            def unit(s_, u, gi, ab):
                sbank = sunits[ucnt[0] % 3]
                sb_i = ucnt[0] % 3
                ucnt[0] += 1
                E.gate("t", ("p2s", sb_i))
                mm = None
                for half in range(2):
                    mm = E.do("t", lambda t, sbank=sbank, b=hb, gi=gi, u=u, half=half:
                              t.matmul(sbank[:, half * 512:(half + 1) * 512],
                                       p2_qt[b][:, gi * 128:(gi + 1) * 128],
                                       p2_kt[b][:, u * 1024 + half * 512:
                                               u * 1024 + (half + 1) * 512],
                                       start=True, stop=True), "pe")
                E.w("v", "pe", mm)
                if u == nu - 1:
                    E.w("v", "msk", mask_ld)
                    mv = E.do("v", lambda v, sbank=sbank, gi=gi: v.tensor_tensor(
                        sbank[:], sbank[:],
                        mask_sb[:, gi * 1024:(gi + 1) * 1024], ALU.add), "dv")
                    E.w("v", "dv", mv)
                r1 = E.do("v", lambda v, sbank=sbank, u=u: v.reduce_max(
                    mtmp[:, u:u + 1], sbank[:], axis=AX.X, negate=True), "dv")
                E.w("s", "dv", r1)
                ev = E.do("s", lambda s, sbank=sbank, ab=ab, u=u: s.activation(
                    p2_att[:, ab + u * 1024: ab + (u + 1) * 1024], sbank[:],
                    AF.Exp, bias=mtmp[:, u:u + 1],
                    accum_out=chsum[:, u:u + 1]), "ac")
                E.mark(("p2s", sb_i), "ac", ev)

            def tail(s_, ab):
                exp_last = E.cnt["ac"]
                E.w("v", "ac", exp_last)
                t1 = E.do("v", lambda v: v.tensor_reduce(
                    rt_m[:], mtmp[:, 0:nu], AX.X, ALU.min), "dv")
                E.w("s", "dv", t1)
                f1 = E.do("s", lambda s: s.activation(
                    fixall[:, 0:nu], mtmp[:, 0:nu], AF.Exp,
                    bias=rt_m[:], scale=-1.0), "ac")
                E.w("v", "ac", f1)
                f2 = E.do("v", lambda v: v.tensor_mul(
                    fsum[:, 0:nu], chsum[:, 0:nu], fixall[:, 0:nu]), "dv")
                E.w("v", "dv", f2)
                f3 = E.do("v", lambda v: v.reduce_sum(
                    st_stot[:], fsum[:, 0:nu], axis=AX.X), "dv")
                E.w("v", "dv", f3)
                f4 = E.do("v", lambda v: v.reciprocal(st_rstd[:], st_stot[:]), "dv")
                E.w("v", "dv", f4)
                f5 = E.do("v", lambda v: v.tensor_scalar(
                    fsum[:, 0:nu], fixall[:, 0:nu], st_rstd[:], None, ALU.mult), "dv")
                E.w("v", "dv", f5)
                for u in range(nu):
                    E.do("v", lambda v, ab=ab, u=u: v.tensor_scalar(
                        p2_att[:, ab + u * 1024: ab + (u + 1) * 1024],
                        p2_att[:, ab + u * 1024: ab + (u + 1) * 1024],
                        fsum[:, u:u + 1], None, ALU.mult), "dv")

            for s_ in range(4):
                gi = 4 * g_ + s_
                ab = attbase(g_, s_)
                for u in range(nu):
                    steps.append(lambda s_=s_, u=u, gi=gi, ab=ab: unit(s_, u, gi, ab))
                steps.append(lambda s_=s_, ab=ab: tail(s_, ab))

            def grp_end():
                gstate[g_] = dict(att_done=E.cnt["dv"], exp_done=E.cnt["ac"])
            steps.append(grp_end)
            return steps

        def cd_steps(g_, hb=hb):
            nu = g_ + 1
            nkt = 8 * nu
            state = dict(ev_of_pair={}, av_last=None)

            def head():
                E.w("t", "dv", max(gstate[g_]["att_done"], vtr_done))
                E.w("t", "ac", gstate[g_]["exp_done"])

            def emit_C(p):
                bank = tbanks[p % 2]
                E.gate("t", ("ptb",))
                E.gate("t", ("p2attT", p))
                last = None
                for k2 in range(2):
                    kt = 2 * p + k2
                    for s_ in range(4):
                        ab = attbase(g_, s_)
                        last = E.do("t", lambda t, bank=bank, k2=k2, s_=s_, kt=kt, ab=ab:
                                    t.transpose(bank[:, k2 * 512 + s_ * 128:
                                                     k2 * 512 + (s_ + 1) * 128],
                                                p2_att[:, ab + kt * 128: ab + (kt + 1) * 128],
                                                ident[:]),
                                    "pe" if (k2 == 1 and s_ == 3) else None)
                eng = "v" if p % 2 == 0 else "s"
                E.w(eng, "pe", last)
                if eng == "v":
                    ev = E.do("v", lambda v, bank=bank, p=p: v.tensor_copy(
                        p2_attT[:, p * 1024:(p + 1) * 1024], bank[:]), "dv")
                    E.mark(("ptb",), "dv", ev)
                    state["ev_of_pair"][p] = ("dv", ev)
                else:
                    ev = E.do("s", lambda s, bank=bank, p=p: s.copy(
                        p2_attT[:, p * 1024:(p + 1) * 1024], bank[:]), "ac")
                    E.mark(("ptb",), "ac", ev)
                    state["ev_of_pair"][p] = ("ac", ev)

            def emit_D(kt):
                sem, val = state["ev_of_pair"][kt // 2]
                E.w("t", sem, val)
                if kt == 0:
                    E.gate("t", ("p2y",))
                state["av_last"] = E.do("t", lambda t, kt=kt, b=hb: t.matmul(
                    PY[:], p2_v[b][:, kt * 128:(kt + 1) * 128],
                    p2_attT[:, kt * 512:(kt + 1) * 512],
                    start=(kt == 0), stop=(kt == nkt - 1)),
                    "pe" if kt == nkt - 1 else None)
                E.mark(("p2attT", kt // 2), "pe", E.cnt["pe"])

            def yevac():
                yb = g_ % 2
                E.gate("v", ("p2yst", yb))
                E.w("v", "pe", state["av_last"])
                ye = E.do("v", lambda v, yb=yb: v.tensor_copy(p2_yst[yb][:], PY[:]), "dv")
                E.mark(("p2y",), "dv", ye)
                E.w("q", "dv", ye)
                yw = E.do("q", lambda q, h_=h_, g_=g_, yb=yb: q.dma_start(
                    out=yt_s[h_, 4 * g_:4 * g_ + 4].rearrange("tt hd q -> hd tt q"),
                    in_=p2_yst[yb][:].rearrange("p (tt q) -> p tt q", q=128)), f"yst{yb}", 16)
                E.mark(("p2yst", yb), f"yst{yb}", yw)
                gstate[g_]["cd_pe"] = E.cnt["pe"]

            steps = [head]
            npair = nkt // 2
            for p in range(npair):
                steps.append(lambda p=p: emit_C(p))
                if p >= 1:
                    steps.append(lambda p=p: (emit_D(2 * (p - 1)), emit_D(2 * p - 1)))
            steps.append(lambda: (emit_D(nkt - 2), emit_D(nkt - 1)))
            steps.append(yevac)
            return steps

        for st in b_steps(0):
            st()
        for g_ in range(4):
            bs = b_steps(g_ + 1) if g_ < 3 else []
            cs = cd_steps(g_)
            nb, ncd = len(bs), len(cs)
            bi = 0
            for ci, c in enumerate(cs):
                c()
                want = min(nb, (ci + 1) * 2 * nb // ncd)
                while bi < want:
                    bs[bi]()
                    bi += 1
            while bi < nb:
                bs[bi]()
                bi += 1
        E.mark(("p2kt", hb), "pe", E.cnt["pe"])
        E.mark(("p2qt",), "pe", E.cnt["pe"])
        E.mark(("p2v",), "pe", E.cnt["pe"])

    E.barrier()

    for i in range(4):
        E.do("g", lambda g, i=i: g.dma_start(
            out=p3_wp[:].rearrange("p (dt c) -> p dt c", c=D)[:, i * 4:(i + 1) * 4, :],
            in_=wproj[i * 4:(i + 1) * 4].rearrange("dt p c -> p dt c")), "wp", 16)
    wp_ld = E.cnt["wp"]
    E.do("q", lambda q: q.dma_start(out=p3_bp[:], in_=bproj.broadcast_to([128, D])), "bp", 16)
    bp_ld = E.cnt["bp"]

    for tt in range(NTT):
        buf = tt % 2
        E.gate("q", ("p3yt", buf))
        yv = E.do("q", lambda q, tt=tt, buf=buf: q.dma_start(
            out=p3_yt[buf][:].rearrange("p (ct q) -> p ct q", q=128),
            in_=yt_s[:, tt].rearrange("ct hd q -> hd ct q")), f"p3yt{buf}", 16)
        E.gate("q", ("p3x", buf))
        xv = E.do("q", lambda q, tt=tt, buf=buf: q.dma_start(
            out=p3_x[buf][:], in_=x_own[tt * 128:(tt + 1) * 128, :]), f"p3x{buf}", 16)
        E.gate("v", ("p3x1", buf))
        last_e2 = None
        for jc in range(4):
            bank_i = jc % 4
            bank = fbanks[bank_i]
            E.gate("t", ("fb", bank_i))
            E.w("t", f"p3yt{buf}", yv)
            E.w("t", "wp", wp_ld)
            last = None
            for ct in range(NDT):
                last = E.do("t", lambda t, bank=bank, buf=buf, ct=ct, jc=jc:
                            t.matmul(bank, p3_yt[buf][:, ct * 128:(ct + 1) * 128],
                                     p3_wp[:, ct * D + jc * 512: ct * D + (jc + 1) * 512],
                                     start=(ct == 0), stop=(ct == NDT - 1)),
                            "pe" if ct == NDT - 1 else None)
            E.w("v", "pe", last)
            E.w("v", "bp", bp_ld)
            E.w("v", f"p3x{buf}", xv)
            e1 = E.do("v", lambda v, bank=bank, buf=buf, jc=jc: v.tensor_add(
                p3_x1[buf][:, jc * 512:(jc + 1) * 512], bank,
                p3_bp[:, jc * 512:(jc + 1) * 512]), "dv")
            E.w("v", "dv", e1)
            e2 = E.do("v", lambda v, buf=buf, jc=jc: v.tensor_add(
                p3_x1[buf][:, jc * 512:(jc + 1) * 512],
                p3_x1[buf][:, jc * 512:(jc + 1) * 512],
                p3_x[buf][:, jc * 512:(jc + 1) * 512]), "dv")
            E.mark(("fb", bank_i), "dv", e2)
            last_e2 = e2
        E.mark(("p3yt", buf), "pe", E.cnt["pe"])
        E.mark(("p3x", buf), "dv", last_e2)
        E.w("q", "dv", last_e2)
        E.do("q", lambda q, tt=tt, buf=buf: q.dma_start(
            out=x1_s[tt], in_=p3_x1[buf][:]), f"xo{buf}", 16)
        s2v = E.do("q", lambda q, tt=tt, buf=buf: q.dma_start(
            out=out[tt * 128:(tt + 1) * 128, :], in_=p3_x1[buf][:]), f"xo{buf}", 16)
        E.mark(("p3x1", buf), f"xo{buf}", s2v)

    E.barrier()

    for tt in range(NTT):
        buf = tt % 2
        E.gate("q", ("p4x", buf))
        xv = E.do("q", lambda q, tt=tt, buf=buf: q.dma_start(
            out=p4_x[buf][:], in_=x1_s[tt]), f"p4x{buf}", 16)
        E.gate("v", ("p4h", buf))
        vn = emit_ln_tile(p4_x[buf][:], f"p4x{buf}", xv, p4_h[buf][:], p4_junk[:])
        E.mark(("p4x", buf), "dv", vn)
        E.w("t", "dv", vn)
        ht_done = None
        for half in range(2):
            bank = tbanks[half]
            E.gate("t", ("ptb",))
            last = None
            for k in range(8):
                dt = half * 8 + k
                last = E.do("t", lambda t, buf=buf, dt=dt, bank=bank, k=k: t.transpose(
                    bank[:, k * 128:(k + 1) * 128],
                    p4_h[buf][:, dt * 128:(dt + 1) * 128], ident[:]),
                    "pe" if k == 7 else None)
            E.w("v", "pe", last)
            E.gate("v", ("p4ht", buf))
            E.gate("v", ("p4htS", buf))
            ev = E.do("v", lambda v, half=half, bank=bank, buf=buf: v.tensor_copy(
                p4_ht[buf][:, half * 1024:(half + 1) * 1024], bank[:]), "dv")
            E.mark(("ptb",), "dv", ev)
            if half == 1:
                E.mark(("p4h", buf), "dv", ev)
                ht_done = ev
        E.w("q", "dv", ht_done)
        hts = E.do("q", lambda q, tt=tt, buf=buf: q.dma_start(
            out=ht_s[:, :, tt * 128:(tt + 1) * 128].rearrange("dt d c -> d dt c"),
            in_=p4_ht[buf][:].rearrange("p (dt c) -> p dt c", c=128)), f"hts{buf}", 16)
        E.mark(("p4htS", buf), f"hts{buf}", hts)
        E.gate("t", ("p4pc",))
        E.w("t", "dv", ht_done)
        E.w("t", "c_gw", const_gld)
        last = None
        for dt in range(NDT):
            last = E.do("t", lambda t, buf=buf, dt=dt: t.matmul(
                PY[:, 0:NE], p4_ht[buf][:, dt * 128:(dt + 1) * 128],
                gwt_sb[:, dt * NE:(dt + 1) * NE],
                start=(dt == 0), stop=(dt == NDT - 1)),
                "pe" if dt == NDT - 1 else None)
        E.mark(("p4ht", buf), "pe", last)
        E.w("v", "pe", last)
        r1 = E.do("v", lambda v: v.reduce_max(rt_m[:], PY[:, 0:NE], axis=AX.X, negate=True), "dv")
        E.w("s", "dv", r1)
        r2 = E.do("s", lambda s: s.activation(rt_sm[:], PY[:, 0:NE], AF.Exp,
                                              bias=rt_m[:], accum_out=rt_s[:]), "ac")
        E.mark(("p4pc",), "ac", r2)
        E.w("v", "ac", r2)
        r3 = E.do("v", lambda v: v.reciprocal(rt_m[:], rt_s[:]), "dv")
        E.w("v", "dv", r3)
        r4 = E.do("v", lambda v, tt=tt: v.tensor_scalar(
            cw_sb[:, tt * NE:(tt + 1) * NE], rt_sm[:], rt_m[:], None, ALU.mult), "dv")
        E.w("v", "dv", r4)
        r5 = E.do("v", lambda v, tt=tt: v.tensor_copy(
            rt_wk[:], cw_sb[:, tt * NE:(tt + 1) * NE]), "dv")
        E.w("v", "dv", r5)
        for it in range(TOPK - 1):
            a1 = E.do("v", lambda v: v.reduce_max(rt_m[:], rt_wk[:], axis=AX.X), "dv")
            E.w("v", "dv", a1)
            a2 = E.do("v", lambda v: v.tensor_scalar(
                rt_ge[:], rt_wk[:], rt_m[:], None, ALU.is_ge), "dv")
            E.w("v", "dv", a2)
            a3 = E.do("v", lambda v: v.scalar_tensor_tensor(
                rt_wk[:], rt_ge[:], -2.0, rt_wk[:], ALU.mult, ALU.add), "dv")
            E.w("v", "dv", a3)
        b1 = E.do("v", lambda v: v.reduce_max(rt_m[:], rt_wk[:], axis=AX.X), "dv")
        E.w("v", "dv", b1)
        b2 = E.do("v", lambda v, tt=tt: v.tensor_scalar(
            rt_ge[:], cw_sb[:, tt * NE:(tt + 1) * NE], rt_m[:], None, ALU.is_ge), "dv")
        E.w("v", "dv", b2)
        E.do("v", lambda v, tt=tt: v.tensor_mul(
            cw_sb[:, tt * NE:(tt + 1) * NE], cw_sb[:, tt * NE:(tt + 1) * NE],
            rt_ge[:]), "dv")

    E.barrier()

    PANELS = [(0, 256), (256, 256), (512, 256), (768, 256), (1024, 256), (1280, 128)]
    POFF = [0, 4096, 8192, 12288, 16384, 20480]
    out_seed0 = E.cnt.get("xo0", 0)
    out_seed1 = E.cnt.get("xo1", 0)
    first_accum = True

    panel_uv = {}

    def emit_panel_pair(gi_, pi):
        ee = gi_ % NE
        m0, mw = PANELS[pi]
        off = POFF[pi]
        pb = pi % 2
        E.gate("g", ("p5wg", pb))
        E.do("g", lambda g, ee=ee, off=off, mw=mw, pb=pb: g.dma_start(
            out=p5_wg[pb][:, 0:NDT * mw],
            in_=wg_t[ee, :, off:off + NDT * mw]), f"p5w{pb}", 16)
        E.gate("g", ("p5wu", pb))
        panel_uv[(gi_, pi)] = E.do("g", lambda g, ee=ee, off=off, mw=mw, pb=pb: g.dma_start(
            out=p5_wu[pb][:, 0:NDT * mw],
            in_=wu_t[ee, :, off:off + NDT * mw]), f"p5w{pb}", 16)

    for ch in range(2):
        E.gate("q", ("p5ht",))
        hv5 = E.do("q", lambda q, ch=ch: q.dma_start(
            out=p5_ht[:].rearrange("p (dt c) -> p dt c", c=1024),
            in_=ht_s[:, :, ch * 1024:(ch + 1) * 1024].rearrange("dt d c -> d dt c")),
            "p5ht", 16)
        for e in range(NE):
            gidx = ch * NE + e
            E.gate("g", ("p5wd", 0))
            wd_v = {}
            wd_v[0] = E.do("g", lambda g, e=e: g.dma_start(
                out=p5_wd[0][:], in_=wd_t[e, :, 0:11264]), "wd0", 16)
            zlast = {}
            for pi, (m0, mw) in enumerate(PANELS):
                pb = pi % 2
                if (gidx, pi) not in panel_uv:
                    emit_panel_pair(gidx, pi)
                uv = panel_uv[(gidx, pi)]
                if pi == 3:
                    E.gate("g", ("p5wd", 1))
                    wd_v[1] = E.do("g", lambda g, e=e: g.dma_start(
                        out=p5_wd[1][:], in_=wd_t[e, :, 11264:22528]), "wd1", 16)
                for mm_j in range(mw // 128):
                    mt = (m0 + mm_j * 128) // 128
                    E.gate("t", ("p5pa",))
                    E.w("t", f"p5w{pb}", uv)
                    E.w("t", "p5ht", hv5)
                    lg = None
                    for dt in range(NDT):
                        wgs = p5_wg[pb][:, dt * mw + mm_j * 128: dt * mw + mm_j * 128 + 128]
                        for th in range(2):
                            lg = E.do("t", lambda t, wgs=wgs, dt=dt, th=th: t.matmul(
                                PA[:, th * 512:(th + 1) * 512], wgs,
                                p5_ht[:, dt * 1024 + th * 512: dt * 1024 + (th + 1) * 512],
                                start=(dt == 0), stop=(dt == NDT - 1)),
                                "pe" if (dt == NDT - 1 and th == 1) else None)
                    sgb = mt % 2
                    E.gate("s", ("p5sg", sgb))
                    E.w("s", "pe", lg)
                    sv = E.do("s", lambda s, sgb=sgb: s.activation(
                        p5_sg[sgb][:], PA[:], AF.Silu), "ac")
                    E.mark(("p5pa",), "ac", sv)
                    E.gate("t", ("p5pb",))
                    lu = None
                    for dt in range(NDT):
                        wus = p5_wu[pb][:, dt * mw + mm_j * 128: dt * mw + mm_j * 128 + 128]
                        for th in range(2):
                            lu = E.do("t", lambda t, wus=wus, dt=dt, th=th: t.matmul(
                                PB[:, th * 512:(th + 1) * 512], wus,
                                p5_ht[:, dt * 1024 + th * 512: dt * 1024 + (th + 1) * 512],
                                start=(dt == 0), stop=(dt == NDT - 1)),
                                "pe" if (dt == NDT - 1 and th == 1) else None)
                    E.gate("v", ("p5z", mt))
                    E.w("v", "ac", sv)
                    E.w("v", "pe", lu)
                    zv = E.do("v", lambda v, sgb=sgb, mt=mt: v.tensor_mul(
                        p5_z[:, mt * 1024:(mt + 1) * 1024], p5_sg[sgb][:], PB[:]), "dv")
                    E.mark(("p5pb",), "dv", zv)
                    E.mark(("p5sg", sgb), "dv", zv)
                    zlast[mt] = zv
                E.mark(("p5wg", pb), "pe", E.cnt["pe"])
                E.mark(("p5wu", pb), "pe", E.cnt["pe"])

            if gidx + 1 < 2 * NE:
                emit_panel_pair(gidx + 1, 0)
                emit_panel_pair(gidx + 1, 1)

            bank_sets = [(PA[:, 0:512], PA[:, 512:1024], ("p5pa",)),
                         (PB[:, 0:512], PB[:, 512:1024], ("p5pb",)),
                         (PS3[:, 0:512], PS3[:, 512:1024], ("p5pcd",))]
            for jh in range(2):
                wb = jh
                dv5 = wd_v[jh]
                for ts in range(8):
                    tt = ch * 8 + ts
                    bs = (jh * 8 + ts) % 3
                    bk0, bk1, bres = bank_sets[bs]
                    E.gate("t", bres)
                    E.w("t", f"wd{jh}", dv5)
                    l1 = None
                    for mt in range(MT):
                        E.w("t", "dv", zlast[mt])
                        E.do("t", lambda t, mt=mt, ts=ts, bk0=bk0, wb=wb: t.matmul(
                            bk0, p5_z[:, mt * 1024 + ts * 128: mt * 1024 + ts * 128 + 128],
                            p5_wd[wb][:, mt * 1024: mt * 1024 + 512],
                            start=(mt == 0), stop=(mt == MT - 1)),
                            None)
                        l1 = E.do("t", lambda t, mt=mt, ts=ts, bk1=bk1, wb=wb: t.matmul(
                            bk1, p5_z[:, mt * 1024 + ts * 128: mt * 1024 + ts * 128 + 128],
                            p5_wd[wb][:, mt * 1024 + 512: mt * 1024 + 1024],
                            start=(mt == 0), stop=(mt == MT - 1)),
                            "pe" if mt == MT - 1 else None)
                    evb = (jh * 8 + ts) % 3
                    E.gate("v", ("p5ev", evb))
                    E.w("v", "pe", l1)
                    cwcol = cw_sb[:, tt * NE + e: tt * NE + e + 1]
                    E.do("v", lambda v, evb=evb, bk0=bk0, cwcol=cwcol: v.tensor_scalar(
                        p5_ev[evb][:, 0:512], bk0, cwcol, None, ALU.mult), "dv")
                    x2v = E.do("v", lambda v, evb=evb, bk1=bk1, cwcol=cwcol: v.tensor_scalar(
                        p5_ev[evb][:, 512:1024], bk1, cwcol, None, ALU.mult), "dv")
                    E.mark(bres, "dv", x2v)
                    E.w("g", "dv", x2v)
                    if first_accum:
                        E.w("g", "xo0", out_seed0)
                        E.w("g", "xo1", out_seed1)
                        first_accum = False
                    av = E.do("g", lambda g, tt=tt, jh=jh, evb=evb: g.dma_start(
                        out=out[tt * 128:(tt + 1) * 128, jh * 1024:(jh + 1) * 1024],
                        in_=p5_ev[evb][:], accum_op=ALU.add), f"gacc{evb}", 16)
                    E.mark(("p5ev", evb), f"gacc{evb}", av)
                E.mark(("p5wd", wb), "pe", E.cnt["pe"])
            for mt in range(MT):
                E.mark(("p5z", mt), "pe", E.cnt["pe"])
        E.mark(("p5ht",), "pe", E.cnt["pe"])

    E.barrier()

    with nc.Block() as blk:
        blk.tensor(lambda t: [f(t) for f in E.ops["t"]])
        blk.vector(lambda v: [f(v) for f in E.ops["v"]])
        blk.scalar(lambda s: [f(s) for f in E.ops["s"]])
        blk.sync(lambda q: [f(q) for f in E.ops["q"]])
        blk.gpsimd(lambda g: [f(g) for f in E.ops["g"]])
    return nc


def make_masks(h_par):
    m = np.zeros((128, NTT * 1024), dtype=np.float32)
    r = np.arange(128)[:, None]
    c = np.arange(1024)[None, :]
    for i in range(NTT):
        jq = 2 * i + h_par
        qg = jq * 128 + r
        ub = i // 4
        kpos = ub * 1024 + c
        pb = kpos // 128
        ob = (pb ^ 1) if h_par == 1 else pb
        korig = ob * 128 + kpos % 128
        vis = korig <= qg
        m[:, i * 1024:(i + 1) * 1024] = np.where(vis, 0.0, -1e30).astype(np.float32)
    return m.astype(BF)


def kernel(L0_attn_x, L0_attn_w_qkv, L0_attn_b_qkv, L0_attn_w_proj, L0_attn_b_proj,
           L0_ffn_gate_w, L0_ffn_w_gate, L0_ffn_w_up, L0_ffn_w_down):
    x = np.ascontiguousarray(np.asarray(L0_attn_x, dtype=np.float32))
    wqkv = np.ascontiguousarray(
        np.asarray(L0_attn_w_qkv, dtype=np.float32).reshape(NDT, 128, 24, 256)
        .transpose(1, 2, 0, 3).reshape(128, 24 * NDT * 256))
    bqkv = np.ascontiguousarray(np.asarray(L0_attn_b_qkv, dtype=np.float32).reshape(48, 128).T)
    wproj = np.ascontiguousarray(np.asarray(L0_attn_w_proj, dtype=np.float32)).reshape(NDT, 128, D)
    bproj = np.ascontiguousarray(np.asarray(L0_attn_b_proj, dtype=np.float32).reshape(1, D))
    gate_w = np.asarray(L0_ffn_gate_w, dtype=np.float32)
    gwt = np.ascontiguousarray(
        gate_w.T.reshape(NDT, 128, NE).transpose(1, 0, 2).reshape(128, NDT * NE))
    PANELS = [(0, 256), (256, 256), (512, 256), (768, 256), (1024, 256), (1280, 128)]
    def pack_gu(w):
        w3 = np.asarray(w, dtype=np.float32).transpose(0, 2, 1).reshape(NE, NDT, 128, MD)
        return np.ascontiguousarray(np.concatenate(
            [w3[:, :, :, m0:m0 + pw].transpose(0, 2, 1, 3).reshape(NE, 128, NDT * pw)
             for m0, pw in PANELS], axis=2))
    wg_t = pack_gu(L0_ffn_w_gate)
    wu_t = pack_gu(L0_ffn_w_up)
    wd3 = np.asarray(L0_ffn_w_down, dtype=np.float32).transpose(0, 2, 1).reshape(NE, MT, 128, D)
    wd_t = np.ascontiguousarray(np.concatenate(
        [wd3[:, :, :, jh * 1024:(jh + 1) * 1024].transpose(0, 2, 1, 3).reshape(NE, 128, MT * 1024)
         for jh in range(2)], axis=2))
    ident = np.eye(128, dtype=BF)

    nc = build_program()

    swap = np.arange(T // 128).reshape(-1, 2)[:, ::-1].reshape(-1)
    in_maps = []
    for core in range(8):
        b, h_par = core // 2, core % 2
        xb = x[b]
        if h_par == 1:
            x_perm = np.ascontiguousarray(xb.reshape(T // 128, 128, D)[swap].reshape(T, D))
        else:
            x_perm = xb
        x_o = np.ascontiguousarray(
            xb.reshape(T // 128, 128, D)[h_par::2].reshape(OWN, D))
        in_maps.append({
            "x_b": x_perm, "x_own": x_o,
            "wqkv": wqkv, "bqkv": bqkv, "wproj": wproj, "bproj": bproj,
            "gwt": gwt, "wg_t": wg_t, "wu_t": wu_t, "wd_t": wd_t,
            "masks": make_masks(h_par), "ident": ident,
        })
    res = run_bass_kernel_spmd(nc, in_maps, core_ids=list(range(8)))

    full = np.empty((B, T, D), dtype=np.float32)
    for core in range(8):
        b, h_par = core // 2, core % 2
        o = np.asarray(res.results[core]["out"])
        for tt in range(NTT):
            full[b, (2 * tt + h_par) * 128:(2 * tt + h_par + 1) * 128, :] = \
                o[tt * 128:(tt + 1) * 128, :]
    return full
